# version 01
# baseline (speedup 1.0000x reference)
import contextlib
import numpy as np
import ml_dtypes
import concourse.bass as bass
import concourse.mybir as mybir
from concourse.bass_utils import run_bass_kernel_spmd

F32 = mybir.dt.float32
BF16 = mybir.dt.bfloat16
AF = mybir.ActivationFunctionType
ALU = mybir.AluOpType
AX = mybir.AxisListType

B = 2
S = 8192
D = 2048
H = 16
HL = 4
DK = 128
NCH = D // 128
SBK = 512
NSB = S // SBK
QT = 2048
NBLK = 4
NKT = S // 128
CKT = 16
CEXP = 16.0
EPS = 1e-5
SCL = 1.0 / np.sqrt(DK)
LORA_SCALE = 2.0


class Sem:
    def __init__(self, nc, es, name):
        self.h = es.enter_context(nc.semaphore(name))
        self.n = 0

    def inc(self, inst, k=1):
        inst.then_inc(self.h, k)
        self.n += k
        return self.n


def build_program():
    nc = bass.Bass("TRN2", target_bir_lowering=False, debug=False)

    x_in = nc.dram_tensor("x", [S, D], F32, kind="ExternalInput")
    xq_in = nc.dram_tensor("xq", [QT, D], F32, kind="ExternalInput")
    wqT = nc.dram_tensor("wqT", [D, HL * DK], F32, kind="ExternalInput")
    wkT = nc.dram_tensor("wkT", [D, HL * DK], F32, kind="ExternalInput")
    wvT = nc.dram_tensor("wvT", [D, HL * DK], F32, kind="ExternalInput")
    woT = nc.dram_tensor("woT", [D, D], F32, kind="ExternalInput")
    wbT = nc.dram_tensor("wbT", [D, D], F32, kind="ExternalInput")
    awrT = nc.dram_tensor("awrT", [D, 72], F32, kind="ExternalInput")
    bflT = nc.dram_tensor("bflT", [64, D], F32, kind="ExternalInput")
    bbase = nc.dram_tensor("bbase", [128, 16], F32, kind="ExternalInput")
    ident_in = nc.dram_tensor("ident", [128, 128], BF16, kind="ExternalInput")
    identf_in = nc.dram_tensor("identf", [128, 128], F32, kind="ExternalInput")
    pswap_in = nc.dram_tensor("pswap", [128, 128], BF16, kind="ExternalInput")
    e8_in = nc.dram_tensor("e8", [8, 64], BF16, kind="ExternalInput")
    cc_tab = nc.dram_tensor("cctab", [128, S], BF16, kind="ExternalInput")
    ss2_tab = nc.dram_tensor("ss2tab", [128, S], BF16, kind="ExternalInput")
    masks_in = nc.dram_tensor("masks", [4 * 128, 512], BF16, kind="ExternalInput")

    out_q = nc.dram_tensor("out_q", [QT, D], F32, kind="ExternalOutput")

    qsp = [nc.dram_tensor(f"qsp{h}", [128, NSB, SBK], BF16) for h in range(HL)]
    ksp = [nc.dram_tensor(f"ksp{h}", [128, NKT, 128], BF16) for h in range(HL)]
    vsp = [nc.dram_tensor(f"vsp{h}", [128, NKT, 128], BF16) for h in range(HL)]
    ccg_in = [nc.dram_tensor(f"ccin{k}", [HL * DK, QT], BF16) for k in range(4)]
    ccg_out = [nc.dram_tensor(f"ccout{k}", [4 * HL * DK, QT], BF16) for k in range(4)]
    r1sp = nc.dram_tensor("r1sp", [QT, D], F32)

    es = contextlib.ExitStack()
    with es:
        sb = lambda name, shape, dt: es.enter_context(nc.sbuf_tensor(name, shape, dt))

        ids = sb("ids", [128, 128], BF16)
        idsf = sb("idsf", [128, 128], F32)
        psw = sb("psw", [128, 128], BF16)
        e8s = sb("e8s", [8, 64], BF16)
        onecol = sb("onecol", [128, 1], BF16)
        onerow = sb("onerow", [1, 128], F32)
        cexp = sb("cexp", [128, 1], F32)
        epsb = sb("epsb", [128, 1], F32)
        zerob = sb("zerob", [128, 1], F32)
        bnst = sb("bnst", [128, 24], F32)
        mv = sb("mv", [128, 2], F32)
        rstd = sb("rstd", [128, 1], F32)
        lnv = sb("lnv", [128, 1], F32)

        s_v = Sem(nc, es, "s_v")
        s_s = Sem(nc, es, "s_s")
        s_cst = Sem(nc, es, "s_cst")
        s_ldx = Sem(nc, es, "s_ldx")
        s_wld = Sem(nc, es, "s_wld")
        s_x = [Sem(nc, es, f"s_x{i}") for i in range(2)]
        s_spq = [Sem(nc, es, f"s_spq{i}") for i in range(2)]
        s_spv = [Sem(nc, es, f"s_spv{i}") for i in range(2)]
        s_q = [Sem(nc, es, f"s_q{i}") for i in range(2)]
        s_kv = [Sem(nc, es, f"s_kv{i}") for i in range(2)]
        s_sto = [Sem(nc, es, f"s_sto{i}") for i in range(2)]
        s_xq = [Sem(nc, es, f"s_xq{i}") for i in range(2)]
        s_sr = [Sem(nc, es, f"s_sr{i}") for i in range(2)]
        s_ra = [Sem(nc, es, f"s_ra{i}") for i in range(2)]
        s_rc = [Sem(nc, es, f"s_rc{i}") for i in range(2)]
        s_so = [Sem(nc, es, f"s_so{i}") for i in range(2)]
        s_ln = Sem(nc, es, "s_ln")
        s_rs = Sem(nc, es, "s_rs")
        s_ln2 = Sem(nc, es, "s_ln2")
        s_tp = Sem(nc, es, "s_tp")
        s_evT = Sem(nc, es, "s_evT")
        s_proj = Sem(nc, es, "s_proj")
        s_pe = Sem(nc, es, "s_pe")
        s_sw = Sem(nc, es, "s_sw")
        s_rope = Sem(nc, es, "s_rope")
        s_vev = Sem(nc, es, "s_vev")
        s_mm = Sem(nc, es, "s_mm")
        s_exp = Sem(nc, es, "s_exp")
        s_msk = Sem(nc, es, "s_msk")
        s_av = Sem(nc, es, "s_av")
        s_zi = Sem(nc, es, "s_zi")
        s_bc = Sem(nc, es, "s_bc")
        s_o = Sem(nc, es, "s_o")
        s_cc = Sem(nc, es, "s_cc")
        s_lda = Sem(nc, es, "s_lda")
        s_mm3 = Sem(nc, es, "s_mm3")
        s_yev = Sem(nc, es, "s_yev")
        s_r1 = Sem(nc, es, "s_r1")
        s_g = Sem(nc, es, "s_g")

        with contextlib.ExitStack() as es1:
            sb1 = lambda n_, sh, dt: es1.enter_context(nc.sbuf_tensor(n_, sh, dt))
            ps1 = lambda n_, sh, dt: es1.enter_context(nc.psum_tensor(n_, sh, dt))
            xs = sb1("xs", [128, 2, D], F32)
            lnb = sb1("lnb", [128, D], BF16)
            lnT = sb1("lnT", [128, NCH, SBK], BF16)
            wq_s = sb1("wq_s", [128, NCH, HL * DK], BF16)
            wk_s = sb1("wk_s", [128, NCH, HL * DK], BF16)
            wv_s = sb1("wv_s", [128, NCH, HL * DK], BF16)
            ccs = sb1("ccs", [128, S], BF16)
            ss2s = sb1("ss2s", [128, S], BF16)
            pre = sb1("pre", [128, SBK], BF16)
            swp = sb1("swp", [128, SBK], BF16)
            tmpA = sb1("tmpA", [128, SBK], BF16)
            ropo = sb1("ropo", [128, 2, SBK], BF16)
            vstg = sb1("vstg", [128, 2, HL * DK], BF16)
            pT = ps1("pT", [128, 1024], BF16)
            pQK = ps1("pQK", [128, 512], F32)
            pSw = ps1("pSw", [128, 512], F32)
            pV = ps1("pV", [128, 512], F32)

            blk1 = es1.enter_context(nc.Block("p1"))

            def ld_init(sync):
                s_ldx.inc(sync.dma_start(out=ids[:, :], in_=ident_in[:, :]), 16)
                s_ldx.inc(sync.dma_start(out=idsf[:, :], in_=identf_in[:, :]), 16)
                s_ldx.inc(sync.dma_start(out=psw[:, :], in_=pswap_in[:, :]), 16)
                s_ldx.inc(sync.dma_start(out=e8s[:, :], in_=e8_in[:, :]), 16)
                s_ldx.inc(sync.dma_start(out=ccs[:, :], in_=cc_tab[:, :]), 16)
                s_ldx.inc(sync.dma_start(out=ss2s[:, :], in_=ss2_tab[:, :]), 16)
            blk1.sync(ld_init)

            def ld_w(g):
                for w_d, w_s in ((wqT, wq_s), (wkT, wk_s), (wvT, wv_s)):
                    w_r = w_d.rearrange("(c p) o -> c p o", p=128)
                    for c in range(NCH):
                        s_wld.inc(g.dma_start(out=w_s[:, c, :], in_=w_r[c]), 16)
            blk1.gpsimd(ld_w)
            n_wld1 = s_wld.n
            n_ldi1 = s_ldx.n

            def consts(v):
                v.memset(onecol[:, :], 1.0)
                v.memset(onerow[:, :], 1.0)
                v.memset(cexp[:, :], -CEXP)
                v.memset(epsb[:, :], EPS)
                s_cst.inc(v.memset(zerob[:, :], 0.0))
                v.wait_ge(s_ldx.h, n_ldi1)
                v.wait_ge(s_wld.h, n_wld1)
            blk1.vector(consts)

            def eng_gate1(e):
                e.wait_ge(s_cst.h, 1)
                e.wait_ge(s_ldx.h, n_ldi1)
                e.wait_ge(s_wld.h, n_wld1)
            blk1.scalar(eng_gate1)
            blk1.tensor(eng_gate1)

            x_r = x_in.rearrange("(g p) d -> g p d", p=128)
            ln2_mark = []
            tp_mark = []
            for s_i in range(NSB):
                for tt in range(4):
                    g = 4 * s_i + tt

                    def ld_x(sync, g=g, need=(ln2_mark[g - 2] if g >= 2 else 0)):
                        if need > 0:
                            sync.wait_ge(s_ln2.h, need)
                        s_x[g % 2].inc(sync.dma_start(out=xs[:, g % 2, :], in_=x_r[g]), 16)
                    blk1.sync(ld_x)

                    def v_ln(v, g=g, need_ld=s_x[g % 2].n, need_rs=s_rs.n + 1,
                             need_tp=(tp_mark[g - 1] if g >= 1 else 0)):
                        v.wait_ge(s_x[g % 2].h, need_ld)
                        for i in range(3):
                            v.bn_stats(bnst[:, i * 6:(i + 1) * 6],
                                       xs[:, g % 2, i * 512:(i + 1) * 512])
                        s_v.inc(v.bn_stats(bnst[:, 18:24], xs[:, g % 2, 1536:2048]))
                        v.wait_ge(s_v.h, s_v.n)
                        s_ln.inc(v.bn_aggr(mv[:, :], bnst[:, :]))
                        v.wait_ge(s_rs.h, need_rs)
                        if need_tp > 0:
                            v.wait_ge(s_tp.h, need_tp)
                        s_ln2.inc(v.tensor_scalar(out=lnb[:, :], in0=xs[:, g % 2, :],
                                                  scalar1=mv[:, 0:1], scalar2=rstd[:, :],
                                                  op0=ALU.subtract, op1=ALU.mult))
                    blk1.vector(v_ln)
                    ln2_mark.append(s_ln2.n)

                    def s_rstd(scl, need=s_ln.n):
                        scl.wait_ge(s_ln.h, need)
                        s_s.inc(scl.activation(out=lnv[:, :], in_=mv[:, 1:2], func=AF.Ln,
                                               bias=epsb[:, :], scale=1.0))
                        scl.wait_ge(s_s.h, s_s.n)
                        s_rs.inc(scl.activation(out=rstd[:, :], in_=lnv[:, :], func=AF.Exp,
                                                bias=zerob[:, :], scale=-0.5))
                    blk1.scalar(s_rstd)

                    for rnd in range(2):
                        def t_tp(t, rnd=rnd, need_ln=s_ln2.n, need_ev=s_evT.n):
                            if rnd == 0:
                                t.wait_ge(s_ln2.h, need_ln)
                            if need_ev > 0:
                                t.wait_ge(s_evT.h, need_ev)
                            for i in range(8):
                                c = rnd * 8 + i
                                mmi = t.transpose(pT[:, i * 128:(i + 1) * 128],
                                                  lnb[:, c * 128:(c + 1) * 128], ids[:, :])
                            s_tp.inc(mmi)
                        blk1.tensor(t_tp)

                        def v_ev(v, tt=tt, rnd=rnd, need=s_tp.n):
                            v.wait_ge(s_tp.h, need)
                            s_evT.inc(v.tensor_copy(
                                lnT[:, rnd * 8:(rnd + 1) * 8, tt * 128:(tt + 1) * 128],
                                pT[:, :].rearrange("p (c f) -> p c f", f=128)))
                        blk1.vector(v_ev)
                    tp_mark.append(s_tp.n)

                def t_sb_gate(t, need=s_evT.n):
                    t.wait_ge(s_evT.h, need)
                blk1.tensor(t_sb_gate)

                for h in range(HL):
                    for which, w_s in (("q", wq_s), ("k", wk_s)):
                        def t_proj(t, h=h, w_s=w_s, need_pe=s_pe.n):
                            if need_pe > 0:
                                t.wait_ge(s_pe.h, need_pe)
                            for c in range(NCH):
                                mmi = t.matmul(pQK[:, :], w_s[:, c, h * 128:(h + 1) * 128],
                                               lnT[:, c, :], start=(c == 0),
                                               stop=(c == NCH - 1))
                            s_proj.inc(mmi)
                        blk1.tensor(t_proj)

                        def v_pe(v, need=s_proj.n, need_rope=s_rope.n):
                            v.wait_ge(s_proj.h, need)
                            if need_rope > 0:
                                v.wait_ge(s_rope.h, need_rope)
                            s_pe.inc(v.tensor_copy(pre[:, :], pQK[:, :]))
                        blk1.vector(v_pe)

                        def t_sw(t, need=s_pe.n, need_rp=s_rope.n):
                            t.wait_ge(s_pe.h, need)
                            if need_rp > 0:
                                t.wait_ge(s_rope.h, need_rp)
                            s_sw.inc(t.matmul(pSw[:, :], psw[:, :], pre[:, :],
                                              start=True, stop=True))
                        blk1.tensor(t_sw)

                        rslot = s_rope.n % 2
                        ridx = s_rope.n

                        def v_rope(v, s_i=s_i, rslot=rslot, need=s_sw.n,
                                   need_wr=(s_spq[rslot].n if ridx >= 2 else 0)):
                            v.wait_ge(s_sw.h, need)
                            v.tensor_copy(swp[:, :], pSw[:, :])
                            s_v.inc(v.tensor_tensor(tmpA[:, :], pre[:, :],
                                                    ccs[:, s_i * SBK:(s_i + 1) * SBK],
                                                    op=ALU.mult))
                            v.wait_ge(s_v.h, s_v.n)
                            s_v.inc(v.tensor_tensor(swp[:, :], swp[:, :],
                                                    ss2s[:, s_i * SBK:(s_i + 1) * SBK],
                                                    op=ALU.mult))
                            v.wait_ge(s_v.h, s_v.n)
                            if need_wr > 0:
                                v.wait_ge(s_spq[rslot].h, need_wr)
                            s_rope.inc(v.tensor_tensor(ropo[:, rslot, :], tmpA[:, :],
                                                       swp[:, :], op=ALU.add))
                        blk1.vector(v_rope)

                        def sc_spill(scl, s_i=s_i, h=h, which=which, rslot=rslot,
                                     need=s_rope.n):
                            scl.wait_ge(s_rope.h, need)
                            if which == "q":
                                dst = qsp[h][:, s_i, :]
                            else:
                                dst = ksp[h][:, 4 * s_i:4 * s_i + 4, :].rearrange(
                                    "p a b -> p (a b)")
                            s_spq[rslot].inc(scl.dma_start(out=dst, in_=ropo[:, rslot, :]), 16)
                        blk1.scalar(sc_spill)

                for tt in range(4):
                    g = 4 * s_i + tt
                    vtile = 4 * s_i + tt

                    def t_v(t, tt=tt, need_vev=s_vev.n):
                        if need_vev > 0:
                            t.wait_ge(s_vev.h, need_vev)
                        for c in range(NCH):
                            mmi = t.matmul(pV[:, :], lnT[:, c, tt * 128:(tt + 1) * 128],
                                           wv_s[:, c, :], start=(c == 0), stop=(c == NCH - 1))
                        s_proj.inc(mmi)
                    blk1.tensor(t_v)

                    vslot = s_vev.n % 2

                    def v_vev(v, vslot=vslot, need=s_proj.n,
                              need_wr=(s_spv[vslot].n if vtile >= 2 else 0)):
                        v.wait_ge(s_proj.h, need)
                        if need_wr > 0:
                            v.wait_ge(s_spv[vslot].h, need_wr)
                        s_vev.inc(v.tensor_copy(vstg[:, vslot, :], pV[:, :]))
                    blk1.vector(v_vev)

                    def sc_vsp(scl, g=g, vslot=vslot, need=s_vev.n):
                        scl.wait_ge(s_vev.h, need)
                        for h in range(HL):
                            s_spv[vslot].inc(scl.dma_start(
                                out=vsp[h][:, g, :],
                                in_=vstg[:, vslot, h * 128:(h + 1) * 128]), 16)
                    blk1.scalar(sc_vsp)

            def p1_fin(scl, a0=s_spq[0].n, a1=s_spq[1].n, b0=s_spv[0].n, b1=s_spv[1].n):
                scl.wait_ge(s_spq[0].h, a0)
                scl.wait_ge(s_spq[1].h, a1)
                scl.wait_ge(s_spv[0].h, b0)
                scl.wait_ge(s_spv[1].h, b1)
            blk1.scalar(p1_fin)

        with contextlib.ExitStack() as es2:
            sb2 = lambda n_, sh, dt: es2.enter_context(nc.sbuf_tensor(n_, sh, dt))
            ps2 = lambda n_, sh, dt: es2.enter_context(nc.psum_tensor(n_, sh, dt))
            msk = sb2("msk", [128, 4, 512], BF16)
            qbuf = sb2("qbuf", [128, 2, 512], BF16)
            kbuf = sb2("kbuf", [128, 2, CKT * 128], BF16)
            vbuf = sb2("vbuf", [128, 2, CKT * 128], BF16)
            pb = sb2("pb", [128, 2, 512], BF16)
            zinv = sb2("zinv", [1, 512], F32)
            binv = sb2("binv", [128, 512], F32)
            oout = sb2("oout", [128, 2, 512], BF16)
            pS0 = ps2("pS0", [128, 512], F32)
            pS1 = ps2("pS1", [128, 512], F32)
            pO = ps2("pO", [128, 512], F32)
            pZ = ps2("pZ", [128, 512], F32)
            pB = ps2("pB", [128, 512], F32)
            pSb = [pS0, pS1]

            blk2 = es2.enter_context(nc.Block("p2"))

            def ld_msk(sync):
                m_r = masks_in.rearrange("(t p) q -> t p q", p=128)
                for i in range(4):
                    s_ldx.inc(sync.dma_start(out=msk[:, i, :], in_=m_r[i]), 16)
            blk2.sync(ld_msk)
            n_mskld = s_ldx.n

            def eng_gate2(e, need=n_mskld):
                e.wait_ge(s_ldx.h, need)
            blk2.vector(eng_gate2)
            blk2.tensor(eng_gate2)
            blk2.scalar(eng_gate2)

            mm_mark_iter = []
            kcons_mark = []
            gchunk = 0
            for qb in range(NSB):
                nkt = 4 * (qb + 1)
                nchunks = (nkt + CKT - 1) // CKT
                for h in range(HL):
                    it = qb * HL + h
                    qslot = it % 2

                    def ld_q(sync, qb=qb, h=h, qslot=qslot,
                             need=(mm_mark_iter[it - 2] if it >= 2 else 0)):
                        if need > 0:
                            sync.wait_ge(s_mm.h, need)
                        s_q[qslot].inc(sync.dma_start(out=qbuf[:, qslot, :],
                                                      in_=qsp[h][:, qb, :]), 16)
                    blk2.sync(ld_q)
                    q_mark = s_q[qslot].n

                    ch_base = gchunk
                    ld_marks = {}

                    def emit_ld_kv(ch):
                        nonlocal gchunk
                        k0 = ch * CKT
                        nk = min(CKT, nkt - k0)
                        gc = ch_base + ch

                        def ld_kv(sync, h=h, k0=k0, nk=nk, gc=gc,
                                  need=(kcons_mark[gc - 2] if gc >= 2 else 0)):
                            if need > 0:
                                sync.wait_ge(s_av.h, need)
                            cs = gc % 2
                            s_kv[cs].inc(sync.dma_start(
                                out=kbuf[:, cs, 0:nk * 128],
                                in_=ksp[h][:, k0:k0 + nk, :].rearrange("p a b -> p (a b)")), 16)
                            s_kv[cs].inc(sync.dma_start(
                                out=vbuf[:, cs, 0:nk * 128],
                                in_=vsp[h][:, k0:k0 + nk, :].rearrange("p a b -> p (a b)")), 16)
                        blk2.sync(ld_kv)
                        ld_marks[ch] = s_kv[gc % 2].n
                        gchunk += 1

                    emit_ld_kv(0)
                    if nchunks > 1:
                        emit_ld_kv(1)

                    o_guard = s_o.n

                    for kt in range(nkt):
                        ch = kt // CKT
                        cs = (ch_base + ch) % 2
                        ko = kt - ch * CKT
                        sslot = kt % 2
                        diag = kt >= 4 * qb
                        lastk = (ko == CKT - 1 or kt == nkt - 1)

                        def t_s(t, kt=kt, ch=ch, cs=cs, ko=ko, sslot=sslot, qslot=qslot,
                                need_ld=ld_marks[ch], need_exp=s_exp.n - 1,
                                o_guard=o_guard, lastk=lastk, q_mark=q_mark):
                            if kt == 0:
                                if o_guard > 0:
                                    t.wait_ge(s_o.h, o_guard)
                                t.wait_ge(s_q[qslot].h, q_mark)
                                t.wait_ge(s_kv[cs].h, need_ld)
                            elif ko == 0:
                                t.wait_ge(s_kv[cs].h, need_ld)
                            if need_exp > 0:
                                t.wait_ge(s_exp.h, need_exp)
                            mmi = t.matmul(pSb[sslot][:, :],
                                           kbuf[:, cs, ko * 128:(ko + 1) * 128],
                                           qbuf[:, qslot, :], start=True, stop=True)
                            s_mm.inc(mmi)
                        blk2.tensor(t_s)

                        def s_e(scl, sslot=sslot, need_mm=s_mm.n, need_av=s_av.n - 1):
                            scl.wait_ge(s_mm.h, need_mm)
                            if need_av > 0:
                                scl.wait_ge(s_av.h, need_av)
                            s_exp.inc(scl.activation(out=pb[:, sslot, :],
                                                     in_=pSb[sslot][:, :], func=AF.Exp,
                                                     bias=cexp[:, :], scale=SCL))
                        blk2.scalar(s_e)

                        if diag:
                            def v_m(v, qb=qb, kt=kt, sslot=sslot, need=s_exp.n):
                                v.wait_ge(s_exp.h, need)
                                s_msk.inc(v.tensor_tensor(pb[:, sslot, :], pb[:, sslot, :],
                                                          msk[:, kt - 4 * qb, :],
                                                          op=ALU.mult))
                            blk2.vector(v_m)

                        def t_zav(t, kt=kt, cs=cs, ko=ko, sslot=sslot, diag=diag,
                                  need_e=s_exp.n, need_m=s_msk.n, nkt=nkt, lastk=lastk):
                            if diag:
                                t.wait_ge(s_msk.h, need_m)
                            else:
                                t.wait_ge(s_exp.h, need_e)
                            t.matmul(pZ[0:1, :], onecol[:, :], pb[:, sslot, :],
                                     start=(kt == 0), stop=(kt == nkt - 1))
                            mmi = t.matmul(pO[:, :], vbuf[:, cs, ko * 128:(ko + 1) * 128],
                                           pb[:, sslot, :], start=(kt == 0),
                                           stop=(kt == nkt - 1))
                            s_av.inc(mmi)
                        blk2.tensor(t_zav)
                        if lastk:
                            kcons_mark.append(s_av.n)
                            if ch + 2 < nchunks:
                                emit_ld_kv(ch + 2)

                    mm_mark_iter.append(s_mm.n)

                    def v_zi(v, need=s_av.n):
                        v.wait_ge(s_av.h, need)
                        s_zi.inc(v.reciprocal(zinv[:, :], pZ[0:1, :]))
                    blk2.vector(v_zi)

                    def t_bc(t, need=s_zi.n):
                        t.wait_ge(s_zi.h, need)
                        s_bc.inc(t.matmul(pB[:, :], onerow[:, :], zinv[:, :],
                                          start=True, stop=True))
                    blk2.tensor(t_bc)

                    oslot = it % 2

                    def v_o(v, oslot=oslot, need_bc=s_bc.n,
                            need_st=(s_sto[oslot].n if it >= 2 else 0)):
                        v.wait_ge(s_bc.h, need_bc)
                        s_v.inc(v.tensor_copy(binv[:, :], pB[:, :]))
                        if need_st > 0:
                            v.wait_ge(s_sto[oslot].h, need_st)
                        v.wait_ge(s_v.h, s_v.n)
                        s_o.inc(v.tensor_tensor(oout[:, oslot, :], pO[:, :], binv[:, :],
                                                op=ALU.mult))
                    blk2.vector(v_o)

                    def sc_o(scl, qb=qb, h=h, oslot=oslot, need=s_o.n):
                        scl.wait_ge(s_o.h, need)
                        k = qb // 4
                        s_sto[oslot].inc(scl.dma_start(
                            out=ccg_in[k][h * 128:(h + 1) * 128,
                                          (qb % 4) * 512:(qb % 4 + 1) * 512],
                            in_=oout[:, oslot, :]), 16)
                    blk2.scalar(sc_o)

                if qb % 4 == 3:
                    def g_cc(g, k=qb // 4, need0=s_sto[0].n, need1=s_sto[1].n):
                        g.wait_ge(s_sto[0].h, need0)
                        g.wait_ge(s_sto[1].h, need1)
                        s_cc.inc(g.collective_compute(
                            "AllGather", ALU.bypass,
                            replica_groups=[[0, 1, 2, 3], [4, 5, 6, 7]],
                            ins=[ccg_in[k].ap().opt()],
                            outs=[ccg_out[k].ap().opt()]))
                    blk2.gpsimd(g_cc)

        with contextlib.ExitStack() as es3:
            sb3 = lambda n_, sh, dt: es3.enter_context(nc.sbuf_tensor(n_, sh, dt))
            ps3 = lambda n_, sh, dt: es3.enter_context(nc.psum_tensor(n_, sh, dt))
            wo_s = sb3("wo_s", [128, NCH, D], BF16)
            aT = sb3("aT", [128, NCH, 512], BF16)
            y1T = sb3("y1T", [128, NCH, 512], BF16)
            y1 = sb3("y1", [128, 4, D], BF16)
            xqb = sb3("xqb", [128, 2, D], F32)
            t3 = sb3("t3", [128, D], F32)
            r1b = sb3("r1b", [128, 2, D], F32)
            pY = ps3("pY", [128, 512], F32)
            pT3 = ps3("pT3", [128, 1024], BF16)

            blk3 = es3.enter_context(nc.Block("p3"))

            def ld_wo(g):
                w_r = woT.rearrange("(c p) o -> c p o", p=128)
                for c in range(NCH):
                    s_wld.inc(g.dma_start(out=wo_s[:, c, :], in_=w_r[c]), 16)
            blk3.gpsimd(ld_wo)
            n_wld3 = s_wld.n

            def t_gate3(t, need=n_wld3):
                t.wait_ge(s_wld.h, need)
            blk3.tensor(t_gate3)

            xq_r = xq_in.rearrange("(g p) d -> g p d", p=128)
            r1_r = r1sp.rearrange("(g p) d -> g p d", p=128)

            r1_mark = []
            mm3_mark = []
            for tb in range(NBLK):
                lda_before = s_lda.n

                def sync_aT(sync, tb=tb, lda_before=lda_before,
                            need_mm=(mm3_mark[tb - 1] if tb >= 1 else 0)):
                    if need_mm > 0:
                        sync.wait_ge(s_mm3.h, need_mm)
                    pid = sync.partition_id()
                    for core in nc.Switch(engines=[sync], index=[pid], n=8):
                        tq = core % 4
                        sync.wait_ge(s_cc.h, tq + 1)
                        co = ccg_out[tq]
                        for cb in range(NCH):
                            sync.dma_start(
                                out=aT[:, cb, :],
                                in_=co[cb * 128:(cb + 1) * 128,
                                       tb * 512:(tb + 1) * 512]).then_inc(s_lda.h, 16)
                    s_lda.n = lda_before + 16 * NCH
                blk3.sync(sync_aT)

                for oc in range(NCH):
                    def t_op(t, oc=oc, need_ld=s_lda.n, need_ev=s_yev.n):
                        if oc == 0:
                            t.wait_ge(s_lda.h, need_ld)
                        if need_ev > 0:
                            t.wait_ge(s_yev.h, need_ev)
                        for c in range(NCH):
                            mmi = t.matmul(pY[:, :], wo_s[:, c, oc * 128:(oc + 1) * 128],
                                           aT[:, c, :], start=(c == 0), stop=(c == NCH - 1))
                        s_mm3.inc(mmi)
                    blk3.tensor(t_op)

                    def v_ye(v, oc=oc, need=s_mm3.n):
                        v.wait_ge(s_mm3.h, need)
                        s_yev.inc(v.tensor_copy(y1T[:, oc, :], pY[:, :]))
                    blk3.vector(v_ye)
                mm3_mark.append(s_mm3.n)

                for rnd in range(8):
                    def t_tp3(t, rnd=rnd, need_ev=s_yev.n, need_pt=s_evT.n):
                        if rnd == 0:
                            t.wait_ge(s_yev.h, need_ev)
                        if need_pt > 0:
                            t.wait_ge(s_evT.h, need_pt)
                        for i in range(8):
                            j = rnd * 8 + i
                            oc, sub = j // 4, j % 4
                            mmi = t.transpose(pT3[:, i * 128:(i + 1) * 128],
                                              y1T[:, oc, sub * 128:(sub + 1) * 128],
                                              ids[:, :])
                        s_tp.inc(mmi)
                    blk3.tensor(t_tp3)

                    def v_pt3(v, rnd=rnd, need=s_tp.n):
                        v.wait_ge(s_tp.h, need)
                        oc0 = rnd * 2
                        s_evT.inc(v.tensor_copy(
                            y1[:, :, oc0 * 128:(oc0 + 2) * 128].rearrange(
                                "p s (c f) -> p c s f", f=128),
                            pT3[:, :].rearrange("p (c s f) -> p c s f", c=2, s=4)))
                    blk3.vector(v_pt3)

                for sub in range(4):
                    g = tb * 4 + sub

                    def ld_xq(sync, g=g, need=(r1_mark[g - 2] if g >= 2 else 0)):
                        if need > 0:
                            sync.wait_ge(s_r1.h, need)
                        s_xq[g % 2].inc(sync.dma_start(out=xqb[:, g % 2, :], in_=xq_r[g]), 16)
                    blk3.sync(ld_xq)

                    def v_ln2(v, sub=sub, need_t=s_evT.n, need_rs=s_rs.n + 1):
                        v.wait_ge(s_evT.h, need_t)
                        for i in range(3):
                            v.bn_stats(bnst[:, i * 6:(i + 1) * 6],
                                       y1[:, sub, i * 512:(i + 1) * 512])
                        s_v.inc(v.bn_stats(bnst[:, 18:24], y1[:, sub, 1536:2048]))
                        v.wait_ge(s_v.h, s_v.n)
                        s_ln.inc(v.bn_aggr(mv[:, :], bnst[:, :]))
                        v.wait_ge(s_rs.h, need_rs)
                        s_v.inc(v.tensor_scalar(out=t3[:, :], in0=y1[:, sub, :],
                                                scalar1=mv[:, 0:1], scalar2=rstd[:, :],
                                                op0=ALU.subtract, op1=ALU.mult))
                    blk3.vector(v_ln2)

                    def s_rstd3(scl, need=s_ln.n):
                        scl.wait_ge(s_ln.h, need)
                        s_s.inc(scl.activation(out=lnv[:, :], in_=mv[:, 1:2], func=AF.Ln,
                                               bias=epsb[:, :], scale=1.0))
                        scl.wait_ge(s_s.h, s_s.n)
                        s_rs.inc(scl.activation(out=rstd[:, :], in_=lnv[:, :], func=AF.Exp,
                                                bias=zerob[:, :], scale=-0.5))
                    blk3.scalar(s_rstd3)

                    def v_r1(v, g=g, need_ld=s_xq[g % 2].n,
                             need_wr=(s_sr[g % 2].n if g >= 2 else 0)):
                        v.wait_ge(s_xq[g % 2].h, need_ld)
                        v.wait_ge(s_v.h, s_v.n)
                        if need_wr > 0:
                            v.wait_ge(s_sr[g % 2].h, need_wr)
                        s_r1.inc(v.tensor_tensor(r1b[:, g % 2, :], t3[:, :],
                                                 xqb[:, g % 2, :], op=ALU.add))
                    blk3.vector(v_r1)
                    r1_mark.append(s_r1.n)

                    def sc_r1(scl, g=g, need=s_r1.n):
                        scl.wait_ge(s_r1.h, need)
                        s_sr[g % 2].inc(scl.dma_start(out=r1_r[g], in_=r1b[:, g % 2, :]), 16)
                    blk3.scalar(sc_r1)

            def p3_fin(scl, a0=s_sr[0].n, a1=s_sr[1].n):
                scl.wait_ge(s_sr[0].h, a0)
                scl.wait_ge(s_sr[1].h, a1)
            blk3.scalar(p3_fin)

        with contextlib.ExitStack() as es4:
            sb4 = lambda n_, sh, dt: es4.enter_context(nc.sbuf_tensor(n_, sh, dt))
            ps4 = lambda n_, sh, dt: es4.enter_context(nc.psum_tensor(n_, sh, dt))
            wb_s = sb4("wb_s", [128, NCH, D], BF16)
            awr_s = sb4("awr_s", [128, NCH, 72], BF16)
            bfl_s = sb4("bfl_s", [64, D], BF16)
            bb_s = sb4("bb_s", [128, 16], F32)
            r1a = sb4("r1a", [128, 2, D], F32)
            r1c = sb4("r1c", [128, 2, D], F32)
            ln2 = sb4("ln2", [128, D], BF16)
            ln2T = sb4("ln2T", [128, NCH, 512], BF16)
            hbuf = sb4("hbuf", [128, 512], BF16)
            hbufL = sb4("hbufL", [8, 512], F32)
            lgt = sb4("lgt", [128, 4, 8], F32)
            negm1 = sb4("negm1", [128, 1], F32)
            m2b = sb4("m2b", [128, 1], F32)
            eqb = sb4("eqb", [128, 8], F32)
            lmk = sb4("lmk", [128, 8], F32)
            pexp = sb4("pexp", [128, 8], F32)
            indb = sb4("indb", [128, 8], F32)
            wgt = sb4("wgt", [128, 8], F32)
            zg = sb4("zg", [128, 1], F32)
            zgi = sb4("zgi", [128, 1], F32)
            wn = sb4("wn", [128, 8], F32)
            wnT = sb4("wnT", [8, 128], BF16)
            wexpT = sb4("wexpT", [64, 512], BF16)
            h2 = sb4("h2", [64, 512], BF16)
            ffnT = sb4("ffnT", [128, NCH, 512], BF16)
            ffn = sb4("ffn", [128, 4, D], BF16)
            t4 = sb4("t4", [128, D], F32)
            r2b = sb4("r2b", [128, 2, D], F32)
            pY4 = ps4("pY4", [128, 512], F32)
            pT4 = ps4("pT4", [128, 1024], BF16)
            pH = ps4("pH", [128, 512], F32)
            pG = ps4("pG", [128, 512], F32)

            blk4 = es4.enter_context(nc.Block("p4"))

            def ld_w4(g):
                w_r = wbT.rearrange("(c p) o -> c p o", p=128)
                for c in range(NCH):
                    s_wld.inc(g.dma_start(out=wb_s[:, c, :], in_=w_r[c]), 16)
                a_r = awrT.rearrange("(c p) o -> c p o", p=128)
                for c in range(NCH):
                    s_wld.inc(g.dma_start(out=awr_s[:, c, :], in_=a_r[c]), 16)
                s_wld.inc(g.dma_start(out=bfl_s[:, :], in_=bflT[:, :]), 16)
            blk4.gpsimd(ld_w4)

            def ld_bb(sync):
                s_ldx.inc(sync.dma_start(out=bb_s[:, :], in_=bbase[:, :]), 16)
            blk4.sync(ld_bb)
            n_wld4 = s_wld.n
            n_ld4 = s_ldx.n

            def gate4(e, a=n_wld4, b_=n_ld4):
                e.wait_ge(s_wld.h, a)
                e.wait_ge(s_ldx.h, b_)
            blk4.tensor(gate4)
            blk4.vector(gate4)
            blk4.scalar(gate4)

            out_r = out_q.rearrange("(g p) d -> g p d", p=128)

            ln3_mark = []
            r2_mark = []
            for tb in range(NBLK):
                for sub in range(4):
                    g = tb * 4 + sub

                    def ld_r1(sync, g=g, need=(ln3_mark[g - 2] if g >= 2 else 0)):
                        if need > 0:
                            sync.wait_ge(s_ln2.h, need)
                        s_ra[g % 2].inc(sync.dma_start(out=r1a[:, g % 2, :], in_=r1_r[g]), 16)
                    blk4.sync(ld_r1)

                    def v_ln3(v, g=g, sub=sub, need_ld=s_ra[g % 2].n, need_rs=s_rs.n + 1,
                              need_tp=s_tp.n):
                        v.wait_ge(s_ra[g % 2].h, need_ld)
                        for i in range(3):
                            v.bn_stats(bnst[:, i * 6:(i + 1) * 6],
                                       r1a[:, g % 2, i * 512:(i + 1) * 512])
                        s_v.inc(v.bn_stats(bnst[:, 18:24], r1a[:, g % 2, 1536:2048]))
                        v.wait_ge(s_v.h, s_v.n)
                        s_ln.inc(v.bn_aggr(mv[:, :], bnst[:, :]))
                        v.wait_ge(s_rs.h, need_rs)
                        if need_tp > 0:
                            v.wait_ge(s_tp.h, need_tp)
                        s_ln2.inc(v.tensor_scalar(out=ln2[:, :], in0=r1a[:, g % 2, :],
                                                  scalar1=mv[:, 0:1], scalar2=rstd[:, :],
                                                  op0=ALU.subtract, op1=ALU.mult))
                    blk4.vector(v_ln3)
                    ln3_mark.append(s_ln2.n)

                    def s_rstd4(scl, need=s_ln.n):
                        scl.wait_ge(s_ln.h, need)
                        s_s.inc(scl.activation(out=lnv[:, :], in_=mv[:, 1:2], func=AF.Ln,
                                               bias=epsb[:, :], scale=1.0))
                        scl.wait_ge(s_s.h, s_s.n)
                        s_rs.inc(scl.activation(out=rstd[:, :], in_=lnv[:, :], func=AF.Exp,
                                                bias=zerob[:, :], scale=-0.5))
                    blk4.scalar(s_rstd4)

                    for rnd in range(2):
                        def t_tp4(t, sub=sub, rnd=rnd, need_ln=s_ln2.n, need_ev=s_evT.n,
                                  need_lg=s_lda.n):
                            if rnd == 0:
                                t.wait_ge(s_ln2.h, need_ln)
                                t.wait_ge(s_lda.h, need_lg)
                            if need_ev > 0:
                                t.wait_ge(s_evT.h, need_ev)
                            for i in range(8):
                                c = rnd * 8 + i
                                mmi = t.transpose(pT4[:, i * 128:(i + 1) * 128],
                                                  ln2[:, c * 128:(c + 1) * 128], ids[:, :])
                            s_tp.inc(mmi)
                        blk4.tensor(t_tp4)

                        def v_ev4(v, sub=sub, rnd=rnd, need=s_tp.n):
                            v.wait_ge(s_tp.h, need)
                            s_evT.inc(v.tensor_copy(
                                ln2T[:, rnd * 8:(rnd + 1) * 8, sub * 128:(sub + 1) * 128],
                                pT4[:, :].rearrange("p (c f) -> p c f", f=128)))
                        blk4.vector(v_ev4)

                def t_awr(t, need=s_evT.n, need_g=s_g.n):
                    t.wait_ge(s_evT.h, need)
                    if need_g > 0:
                        t.wait_ge(s_g.h, need_g)
                    for c in range(NCH):
                        mmi = t.matmul(pH[0:72, :], awr_s[:, c, :], ln2T[:, c, :],
                                       start=(c == 0), stop=(c == NCH - 1))
                    s_mm3.inc(mmi)
                blk4.tensor(t_awr)

                def v_hev(v, need=s_mm3.n):
                    v.wait_ge(s_mm3.h, need)
                    s_v.inc(v.tensor_copy(hbuf[0:64, :], pH[0:64, :]))
                    v.wait_ge(s_v.h, s_v.n)
                    s_g.inc(v.tensor_copy(hbufL[:, :], pH[64:72, :]))
                blk4.vector(v_hev)

                for sub in range(4):
                    def t_lg(t, sub=sub, need=s_g.n, need_cons=s_lda.n):
                        if sub == 0:
                            t.wait_ge(s_g.h, need)
                            if need_cons > 0:
                                t.wait_ge(s_lda.h, need_cons)
                        s_tp.inc(t.transpose(pG[:, sub * 8:(sub + 1) * 8],
                                             hbufL[:, sub * 128:(sub + 1) * 128],
                                             idsf[0:8, 0:8]))
                    blk4.tensor(t_lg)

                def v_lg(v, need=s_tp.n):
                    v.wait_ge(s_tp.h, need)
                    s_lda.inc(v.tensor_copy(
                        lgt[:, :, :], pG[:, 0:32].rearrange("p (s e) -> p s e", e=8)))
                blk4.vector(v_lg)

                for sub in range(4):
                    def v_g1(v, sub=sub, need=s_lda.n):
                        if sub == 0:
                            v.wait_ge(s_lda.h, need)
                        l = lgt[:, sub, :]
                        s_v.inc(v.reduce_max(negm1[:, :], l, axis=AX.X, negate=True))
                        v.wait_ge(s_v.h, s_v.n)
                        s_v.inc(v.tensor_scalar(out=eqb[:, :], in0=l, scalar1=negm1[:, :],
                                                scalar2=0.0, op0=ALU.add, op1=ALU.is_ge))
                        v.wait_ge(s_v.h, s_v.n)
                        s_v.inc(v.scalar_tensor_tensor(out=lmk[:, :], in0=eqb[:, :],
                                                       scalar=-30000.0, in1=l,
                                                       op0=ALU.mult, op1=ALU.add))
                        v.wait_ge(s_v.h, s_v.n)
                        s_v.inc(v.reduce_max(m2b[:, :], lmk[:, :], axis=AX.X))
                        v.wait_ge(s_v.h, s_v.n)
                        s_msk.inc(v.tensor_scalar(out=indb[:, :], in0=l, scalar1=m2b[:, :],
                                                  scalar2=None, op0=ALU.is_ge))
                    blk4.vector(v_g1)

                    def s_px(scl, sub=sub, need=s_msk.n):
                        scl.wait_ge(s_msk.h, need)
                        s_rs.inc(scl.activation(out=pexp[:, :], in_=lgt[:, sub, :],
                                                func=AF.Exp, bias=negm1[:, :], scale=1.0))
                    blk4.scalar(s_px)

                    def v_g2(v, need=s_rs.n):
                        v.wait_ge(s_rs.h, need)
                        s_v.inc(v.tensor_tensor(wgt[:, :], indb[:, :], pexp[:, :],
                                                op=ALU.mult))
                        v.wait_ge(s_v.h, s_v.n)
                        s_v.inc(v.reduce_sum(zg[:, :], wgt[:, :], axis=AX.X))
                        v.wait_ge(s_v.h, s_v.n)
                        s_v.inc(v.reciprocal(zgi[:, :], zg[:, :]))
                        v.wait_ge(s_v.h, s_v.n)
                        s_ln.inc(v.tensor_scalar(out=wn[:, :], in0=wgt[:, :],
                                                 scalar1=zgi[:, :], scalar2=LORA_SCALE,
                                                 op0=ALU.mult, op1=ALU.mult))
                    blk4.vector(v_g2)

                    def t_wnT(t, need=s_ln.n, need_ev=s_yev.n):
                        t.wait_ge(s_ln.h, need)
                        if need_ev > 0:
                            t.wait_ge(s_yev.h, need_ev)
                        s_tp.inc(t.transpose(pG[0:8, 128:256], wn[:, :], idsf[:, :]))
                    blk4.tensor(t_wnT)

                    def v_wnT(v, need=s_tp.n):
                        v.wait_ge(s_tp.h, need)
                        s_r1.inc(v.tensor_copy(wnT[:, :], pG[0:8, 128:256]))
                    blk4.vector(v_wnT)

                    def t_e8(t, need=s_r1.n):
                        t.wait_ge(s_r1.h, need)
                        s_bc.inc(t.matmul(pG[0:64, 256:384], e8s[:, :], wnT[:, :],
                                          start=True, stop=True))
                    blk4.tensor(t_e8)

                    def v_we(v, sub=sub, need=s_bc.n):
                        v.wait_ge(s_bc.h, need)
                        s_yev.inc(v.tensor_copy(wexpT[:, sub * 128:(sub + 1) * 128],
                                                pG[0:64, 256:384]))
                    blk4.vector(v_we)

                def v_h2(v, need=s_yev.n):
                    v.wait_ge(s_yev.h, need)
                    s_g.inc(v.tensor_tensor(h2[:, :], hbuf[0:64, :], wexpT[:, :],
                                            op=ALU.mult))
                blk4.vector(v_h2)

                for oc in range(NCH):
                    def t_ffn(t, oc=oc, need_h2=s_g.n, need_ev=s_exp.n):
                        if oc == 0:
                            t.wait_ge(s_g.h, need_h2)
                        if need_ev > 0:
                            t.wait_ge(s_exp.h, need_ev)
                        for c in range(NCH):
                            t.matmul(pY4[:, :], wb_s[:, c, oc * 128:(oc + 1) * 128],
                                     ln2T[:, c, :], start=(c == 0), stop=False)
                        mmi = t.matmul(pY4[:, :], bfl_s[:, oc * 128:(oc + 1) * 128],
                                       h2[:, :], start=False, stop=True)
                        s_mm.inc(mmi)
                    blk4.tensor(t_ffn)

                    def s_fev(scl, oc=oc, need=s_mm.n):
                        scl.wait_ge(s_mm.h, need)
                        s_exp.inc(scl.activation(out=ffnT[:, oc, :], in_=pY4[:, :],
                                                 func=AF.Identity, bias=bb_s[:, oc:oc + 1],
                                                 scale=1.0))
                    blk4.scalar(s_fev)

                for rnd in range(8):
                    def t_tpf(t, rnd=rnd, need_ev=s_exp.n, need_pt=s_evT.n):
                        if rnd == 0:
                            t.wait_ge(s_exp.h, need_ev)
                        if need_pt > 0:
                            t.wait_ge(s_evT.h, need_pt)
                        for i in range(8):
                            j = rnd * 8 + i
                            oc, sub = j // 4, j % 4
                            mmi = t.transpose(pT4[:, i * 128:(i + 1) * 128],
                                              ffnT[:, oc, sub * 128:(sub + 1) * 128],
                                              ids[:, :])
                        s_tp.inc(mmi)
                    blk4.tensor(t_tpf)

                    def v_ptf(v, rnd=rnd, need=s_tp.n):
                        v.wait_ge(s_tp.h, need)
                        oc0 = rnd * 2
                        s_evT.inc(v.tensor_copy(
                            ffn[:, :, oc0 * 128:(oc0 + 2) * 128].rearrange(
                                "p s (c f) -> p c s f", f=128),
                            pT4[:, :].rearrange("p (c s f) -> p c s f", c=2, s=4)))
                    blk4.vector(v_ptf)

                for sub in range(4):
                    g = tb * 4 + sub

                    def ld_r1b(sync, g=g, need=(r2_mark[g - 2] if g >= 2 else 0)):
                        if need > 0:
                            sync.wait_ge(s_o.h, need)
                        s_rc[g % 2].inc(sync.dma_start(out=r1c[:, g % 2, :], in_=r1_r[g]), 16)
                    blk4.sync(ld_r1b)

                    def v_ln4(v, sub=sub, need_t=s_evT.n, need_rs=s_rs.n + 1):
                        v.wait_ge(s_evT.h, need_t)
                        for i in range(3):
                            v.bn_stats(bnst[:, i * 6:(i + 1) * 6],
                                       ffn[:, sub, i * 512:(i + 1) * 512])
                        s_v.inc(v.bn_stats(bnst[:, 18:24], ffn[:, sub, 1536:2048]))
                        v.wait_ge(s_v.h, s_v.n)
                        s_ln.inc(v.bn_aggr(mv[:, :], bnst[:, :]))
                        v.wait_ge(s_rs.h, need_rs)
                        s_v.inc(v.tensor_scalar(out=t4[:, :], in0=ffn[:, sub, :],
                                                scalar1=mv[:, 0:1], scalar2=rstd[:, :],
                                                op0=ALU.subtract, op1=ALU.mult))
                    blk4.vector(v_ln4)

                    def s_rstd5(scl, need=s_ln.n):
                        scl.wait_ge(s_ln.h, need)
                        s_s.inc(scl.activation(out=lnv[:, :], in_=mv[:, 1:2], func=AF.Ln,
                                               bias=epsb[:, :], scale=1.0))
                        scl.wait_ge(s_s.h, s_s.n)
                        s_rs.inc(scl.activation(out=rstd[:, :], in_=lnv[:, :], func=AF.Exp,
                                                bias=zerob[:, :], scale=-0.5))
                    blk4.scalar(s_rstd5)

                    def v_r2(v, g=g, need_ld=s_rc[g % 2].n,
                             need_st=(s_so[g % 2].n if g >= 2 else 0)):
                        v.wait_ge(s_rc[g % 2].h, need_ld)
                        v.wait_ge(s_v.h, s_v.n)
                        if need_st > 0:
                            v.wait_ge(s_so[g % 2].h, need_st)
                        s_o.inc(v.tensor_tensor(r2b[:, g % 2, :], t4[:, :],
                                                r1c[:, g % 2, :], op=ALU.add))
                    blk4.vector(v_r2)
                    r2_mark.append(s_o.n)

                    def sc_out(scl, g=g, need=s_o.n):
                        scl.wait_ge(s_o.h, need)
                        s_so[g % 2].inc(scl.dma_start(out=out_r[g], in_=r2b[:, g % 2, :]), 16)
                    blk4.scalar(sc_out)

            def p4_fin(scl, a0=s_so[0].n, a1=s_so[1].n):
                scl.wait_ge(s_so[0].h, a0)
                scl.wait_ge(s_so[1].h, a1)
            blk4.scalar(p4_fin)

    return nc


def _host_tables():
    half = DK // 2
    inv_freq = 10000.0 ** (-2.0 * np.arange(half) / DK)
    ang = np.arange(S)[:, None] * inv_freq[None, :]
    cosT = np.cos(ang).T.astype(np.float32)
    sinT = np.sin(ang).T.astype(np.float32)
    cc = np.concatenate([cosT, cosT], axis=0).astype(ml_dtypes.bfloat16)
    ss2 = np.concatenate([-sinT, sinT], axis=0).astype(ml_dtypes.bfloat16)
    masks = np.zeros((4 * 128, 512), dtype=ml_dtypes.bfloat16)
    for i in range(4):
        for kk in range(128):
            masks[i * 128 + kk, i * 128 + kk:] = 1.0
    ident = np.eye(128, dtype=ml_dtypes.bfloat16)
    pswap = np.zeros((128, 128), dtype=ml_dtypes.bfloat16)
    for i in range(64):
        pswap[i + 64, i] = 1.0
        pswap[i, i + 64] = 1.0
    e8 = np.zeros((8, 64), dtype=ml_dtypes.bfloat16)
    for e in range(8):
        e8[e, e * 8:(e + 1) * 8] = 1.0
    return cc, ss2, masks, ident, pswap, e8


def _perm_rows(w):
    perm = []
    for h in range(H):
        base = h * DK
        perm += [base + 2 * i for i in range(DK // 2)]
        perm += [base + 2 * i + 1 for i in range(DK // 2)]
    return w[perm, :]


_NC_CACHE = {}


def kernel(L0_attn_x, L0_attn_wq, L0_attn_wk, L0_attn_wv, L0_attn_wo,
           L0_ffn_W_base, L0_ffn_b_base, L0_ffn_W_router, L0_ffn_A_w, L0_ffn_B_w):
    x = np.ascontiguousarray(np.asarray(L0_attn_x, dtype=np.float32))
    wq = np.asarray(L0_attn_wq, dtype=np.float32)
    wk = np.asarray(L0_attn_wk, dtype=np.float32)
    wv = np.asarray(L0_attn_wv, dtype=np.float32)
    wo = np.asarray(L0_attn_wo, dtype=np.float32)
    wb = np.asarray(L0_ffn_W_base, dtype=np.float32)
    bb = np.asarray(L0_ffn_b_base, dtype=np.float32)
    wr = np.asarray(L0_ffn_W_router, dtype=np.float32)
    aw = np.asarray(L0_ffn_A_w, dtype=np.float32)
    bw = np.asarray(L0_ffn_B_w, dtype=np.float32)

    cc, ss2, masks, ident, pswap, e8 = _host_tables()

    wqTf = np.ascontiguousarray(_perm_rows(wq).T)
    wkTf = np.ascontiguousarray(_perm_rows(wk).T)
    wvTf = np.ascontiguousarray(wv.T)
    woTf = np.ascontiguousarray(wo.T)
    wbTf = np.ascontiguousarray(wb.T)
    aflT = aw.reshape(64, D).T
    wrTf = wr.T
    awrTf = np.ascontiguousarray(np.concatenate([aflT, wrTf], axis=1))
    bflTf = np.ascontiguousarray(bw.transpose(0, 2, 1).reshape(64, D))
    bbasef = np.ascontiguousarray(bb.reshape(16, 128).T)

    if "nc" not in _NC_CACHE:
        _NC_CACHE["nc"] = build_program()
    nc = _NC_CACHE["nc"]

    in_maps = []
    for c in range(8):
        b = c // 4
        hg = c % 4
        tq = c % 4
        in_maps.append({
            "x": x[b],
            "xq": np.ascontiguousarray(x[b, tq * QT:(tq + 1) * QT, :]),
            "wqT": np.ascontiguousarray(wqTf[:, hg * 512:(hg + 1) * 512]),
            "wkT": np.ascontiguousarray(wkTf[:, hg * 512:(hg + 1) * 512]),
            "wvT": np.ascontiguousarray(wvTf[:, hg * 512:(hg + 1) * 512]),
            "woT": woTf, "wbT": wbTf, "awrT": awrTf, "bflT": bflTf, "bbase": bbasef,
            "ident": ident, "identf": np.eye(128, dtype=np.float32),
            "pswap": pswap, "e8": e8,
            "cctab": cc, "ss2tab": ss2, "masks": masks,
        })
    res = run_bass_kernel_spmd(nc, in_maps, core_ids=list(range(8)))
    out = np.empty((B, S, D), dtype=np.float32)
    for c in range(8):
        b = c // 4
        tq = c % 4
        out[b, tq * QT:(tq + 1) * QT, :] = res.results[c]["out_q"]
    return out


# version 02
# speedup vs baseline: 1.2949x; 1.2949x over previous
import contextlib
import numpy as np
import ml_dtypes
import concourse.bass as bass
import concourse.mybir as mybir
from concourse.bass_utils import run_bass_kernel_spmd

F32 = mybir.dt.float32
I32 = mybir.dt.int32
BF16 = mybir.dt.bfloat16
AF = mybir.ActivationFunctionType
ALU = mybir.AluOpType
AX = mybir.AxisListType

B = 2
S = 8192
D = 2048
H = 16
HL = 4
DK = 128
NCH = D // 128
SBK = 512
NSB = S // SBK
QT = 2048
NBLK = 4
NKT = S // 128
CKT = 16
CEXP = 16.0
EPS = 1e-5
SCL = 1.0 / np.sqrt(DK)
LORA_SCALE = 2.0
RSQRT_MAGIC = 1597463007.0


class Sem:
    def __init__(self, nc, es, name):
        self.h = es.enter_context(nc.semaphore(name))
        self.n = 0

    def reserve(self, k=1):
        self.n += k
        return self.n


class Emit:
    ENGS = ("sync", "gpsimd", "tensor", "scalar", "vector")

    def __init__(self):
        self.q = {e: [] for e in self.ENGS}

    def add(self, eng, fn):
        self.q[eng].append(fn)

    def flush(self, block):
        for e in self.ENGS:
            fns = self.q[e]
            self.q[e] = []
            if fns:
                def body(hdl, fns=fns):
                    for f in fns:
                        f(hdl)
                getattr(block, e)(body)


def build_program():
    nc = bass.Bass("TRN2", target_bir_lowering=False, debug=False)

    x_in = nc.dram_tensor("x", [S, D], F32, kind="ExternalInput")
    xq_in = nc.dram_tensor("xq", [QT, D], F32, kind="ExternalInput")
    wqT = nc.dram_tensor("wqT", [D, HL * DK], F32, kind="ExternalInput")
    wkT = nc.dram_tensor("wkT", [D, HL * DK], F32, kind="ExternalInput")
    wvT = nc.dram_tensor("wvT", [D, HL * DK], F32, kind="ExternalInput")
    woT = nc.dram_tensor("woT", [D, D], F32, kind="ExternalInput")
    wbT = nc.dram_tensor("wbT", [D, D], F32, kind="ExternalInput")
    awrT = nc.dram_tensor("awrT", [D, 72], F32, kind="ExternalInput")
    bflT = nc.dram_tensor("bflT", [64, D], F32, kind="ExternalInput")
    bbase = nc.dram_tensor("bbase", [128, 16], F32, kind="ExternalInput")
    ident_in = nc.dram_tensor("ident", [128, 128], BF16, kind="ExternalInput")
    identf_in = nc.dram_tensor("identf", [128, 128], F32, kind="ExternalInput")
    pswap_in = nc.dram_tensor("pswap", [128, 128], BF16, kind="ExternalInput")
    e8_in = nc.dram_tensor("e8", [8, 64], BF16, kind="ExternalInput")
    cc_tab = nc.dram_tensor("cctab", [128, S], BF16, kind="ExternalInput")
    ss2_tab = nc.dram_tensor("ss2tab", [128, S], BF16, kind="ExternalInput")
    masks_in = nc.dram_tensor("masks", [4 * 128, 512], BF16, kind="ExternalInput")

    out_q = nc.dram_tensor("out_q", [QT, D], F32, kind="ExternalOutput")

    qsp = [nc.dram_tensor(f"qsp{h}", [128, NSB, SBK], BF16) for h in range(HL)]
    ksp = [nc.dram_tensor(f"ksp{h}", [128, NKT, 128], BF16) for h in range(HL)]
    vsp = [nc.dram_tensor(f"vsp{h}", [128, NKT, 128], BF16) for h in range(HL)]
    ccg_in = [nc.dram_tensor(f"ccin{k}", [HL * DK, QT], BF16) for k in range(4)]
    ccg_out = [nc.dram_tensor(f"ccout{k}", [4 * HL * DK, QT], BF16) for k in range(4)]
    r1sp = nc.dram_tensor("r1sp", [QT, D], F32)

    es = contextlib.ExitStack()
    with es:
        sb = lambda name, shape, dt: es.enter_context(nc.sbuf_tensor(name, shape, dt))

        ids = sb("ids", [128, 128], BF16)
        idsf = sb("idsf", [128, 128], F32)
        psw = sb("psw", [128, 128], BF16)
        e8s = sb("e8s", [8, 64], BF16)
        onecol = sb("onecol", [128, 1], BF16)
        onerow = sb("onerow", [1, 128], F32)
        cexp = sb("cexp", [128, 1], F32)
        bnst = sb("bnst", [128, 24], F32)
        mv = sb("mv", [128, 2], F32)
        rstd = sb("rstd", [128, 1], F32)
        rs_t1 = sb("rs_t1", [128, 1], F32)
        rs_t2 = sb("rs_t2", [128, 1], F32)

        s_v = Sem(nc, es, "s_v")
        s_cst = Sem(nc, es, "s_cst")
        s_ldx = Sem(nc, es, "s_ldx")
        s_wld = Sem(nc, es, "s_wld")
        s_x = [Sem(nc, es, f"s_x{i}") for i in range(2)]
        s_spq = [Sem(nc, es, f"s_spq{i}") for i in range(2)]
        s_spv = [Sem(nc, es, f"s_spv{i}") for i in range(2)]
        s_q = [Sem(nc, es, f"s_q{i}") for i in range(2)]
        s_kv = [Sem(nc, es, f"s_kv{i}") for i in range(2)]
        s_sto = [Sem(nc, es, f"s_sto{i}") for i in range(2)]
        s_xq = [Sem(nc, es, f"s_xq{i}") for i in range(2)]
        s_sr = [Sem(nc, es, f"s_sr{i}") for i in range(2)]
        s_ra = [Sem(nc, es, f"s_ra{i}") for i in range(2)]
        s_rc = [Sem(nc, es, f"s_rc{i}") for i in range(2)]
        s_so = [Sem(nc, es, f"s_so{i}") for i in range(2)]
        s_ln = Sem(nc, es, "s_ln")
        s_ln2 = Sem(nc, es, "s_ln2")
        s_tp = Sem(nc, es, "s_tp")
        s_evT = Sem(nc, es, "s_evT")
        s_proj = Sem(nc, es, "s_proj")
        s_pe = Sem(nc, es, "s_pe")
        s_sw = Sem(nc, es, "s_sw")
        s_rope = Sem(nc, es, "s_rope")
        s_vev = Sem(nc, es, "s_vev")
        s_mm = Sem(nc, es, "s_mm")
        s_exp = Sem(nc, es, "s_exp")
        s_msk = Sem(nc, es, "s_msk")
        s_av = Sem(nc, es, "s_av")
        s_zi = Sem(nc, es, "s_zi")
        s_bc = Sem(nc, es, "s_bc")
        s_o = Sem(nc, es, "s_o")
        s_cc = Sem(nc, es, "s_cc")
        s_lda = Sem(nc, es, "s_lda")
        s_mm3 = Sem(nc, es, "s_mm3")
        s_yev = Sem(nc, es, "s_yev")
        s_r1 = Sem(nc, es, "s_r1")
        s_g = Sem(nc, es, "s_g")
        s_rs = Sem(nc, es, "s_rs")

        def queue_rsqrt(E):
            r = [s_v.reserve() for _ in range(9)]
            dn = s_rs.reserve()

            def body(v, r=r, dn=dn):
                v.tensor_scalar(out=rs_t1[:, :], in0=mv[:, 1:2], scalar1=EPS,
                                scalar2=None, op0=ALU.add).then_inc(s_v.h, 1)
                v.wait_ge(s_v.h, r[0])
                v.tensor_scalar(out=rs_t2[:, :], in0=rs_t1[:, :], scalar1=-0.5,
                                scalar2=None, op0=ALU.mult).then_inc(s_v.h, 1)
                v.wait_ge(s_v.h, r[1])
                v.tensor_scalar(out=rstd[:, :].bitcast(I32),
                                in0=rs_t1[:, :].bitcast(I32), scalar1=1,
                                scalar2=None,
                                op0=ALU.logical_shift_right).then_inc(s_v.h, 1)
                v.wait_ge(s_v.h, r[2])
                v.tensor_scalar(out=rstd[:, :].bitcast(I32),
                                in0=rstd[:, :].bitcast(I32), scalar1=-1.0,
                                scalar2=RSQRT_MAGIC, op0=ALU.mult,
                                op1=ALU.add).then_inc(s_v.h, 1)
                v.wait_ge(s_v.h, r[3])
                k = 4
                for it in range(2):
                    v.tensor_tensor(rs_t1[:, :], rstd[:, :], rstd[:, :],
                                    op=ALU.mult).then_inc(s_v.h, 1)
                    v.wait_ge(s_v.h, r[k]); k += 1
                    v.tensor_scalar(out=rs_t1[:, :], in0=rs_t1[:, :],
                                    scalar1=rs_t2[:, :], scalar2=1.5,
                                    op0=ALU.mult, op1=ALU.add).then_inc(s_v.h, 1)
                    v.wait_ge(s_v.h, r[k]); k += 1
                    inst = v.tensor_tensor(rstd[:, :], rstd[:, :], rs_t1[:, :],
                                           op=ALU.mult)
                    if it == 1:
                        inst.then_inc(s_rs.h, 1)
                    else:
                        inst.then_inc(s_v.h, 1)
                        v.wait_ge(s_v.h, r[k]); k += 1
            E.add("vector", body)
            return dn

        def queue_ln(E, src_fn, out_fn):
            t_stats = s_v.reserve()
            t_aggr = s_ln.reserve()

            def v_stats(v, src_fn=src_fn, t_stats=t_stats, t_aggr=t_aggr):
                src = src_fn()
                for i in range(3):
                    v.bn_stats(bnst[:, i * 6:(i + 1) * 6], src[:, i * 512:(i + 1) * 512])
                v.bn_stats(bnst[:, 18:24], src[:, 1536:2048]).then_inc(s_v.h, 1)
                v.wait_ge(s_v.h, t_stats)
                v.bn_aggr(mv[:, :], bnst[:, :]).then_inc(s_ln.h, 1)
                v.wait_ge(s_ln.h, t_aggr)
            E.add("vector", v_stats)
            dn = queue_rsqrt(E)
            t_norm = s_ln2.reserve()

            def v_norm(v, src_fn=src_fn, out_fn=out_fn, dn=dn):
                v.wait_ge(s_rs.h, dn)
                v.tensor_scalar(out=out_fn(), in0=src_fn(),
                                scalar1=mv[:, 0:1], scalar2=rstd[:, :],
                                op0=ALU.subtract, op1=ALU.mult).then_inc(s_ln2.h, 1)
            E.add("vector", v_norm)
            return t_norm

        with contextlib.ExitStack() as es1:
            sb1 = lambda n_, sh, dt: es1.enter_context(nc.sbuf_tensor(n_, sh, dt))
            ps1 = lambda n_, sh, dt: es1.enter_context(nc.psum_tensor(n_, sh, dt))
            xs = sb1("xs", [128, 2, D], F32)
            lnb = sb1("lnb", [128, D], BF16)
            lnT = sb1("lnT", [128, NCH, SBK], BF16)
            wq_s = sb1("wq_s", [128, NCH, HL * DK], BF16)
            wk_s = sb1("wk_s", [128, NCH, HL * DK], BF16)
            wv_s = sb1("wv_s", [128, NCH, HL * DK], BF16)
            ccs = sb1("ccs", [128, S], BF16)
            ss2s = sb1("ss2s", [128, S], BF16)
            pre = sb1("pre", [128, SBK], BF16)
            swp = sb1("swp", [128, SBK], BF16)
            tmpA = sb1("tmpA", [128, SBK], BF16)
            ropo = sb1("ropo", [128, 2, SBK], BF16)
            vstg = sb1("vstg", [128, 2, HL * DK], BF16)
            pT = ps1("pT", [128, 1024], BF16)
            pQK = ps1("pQK", [128, 512], F32)
            pSw = ps1("pSw", [128, 512], F32)
            pV = ps1("pV", [128, 512], F32)

            blk1 = es1.enter_context(nc.Block("p1"))
            E = Emit()

            n_ldi1 = s_ldx.reserve(16 * 6)

            def ld_init(sync):
                sync.dma_start(out=ids[:, :], in_=ident_in[:, :]).then_inc(s_ldx.h, 16)
                sync.dma_start(out=idsf[:, :], in_=identf_in[:, :]).then_inc(s_ldx.h, 16)
                sync.dma_start(out=psw[:, :], in_=pswap_in[:, :]).then_inc(s_ldx.h, 16)
                sync.dma_start(out=e8s[:, :], in_=e8_in[:, :]).then_inc(s_ldx.h, 16)
                sync.dma_start(out=ccs[:, :], in_=cc_tab[:, :]).then_inc(s_ldx.h, 16)
                sync.dma_start(out=ss2s[:, :], in_=ss2_tab[:, :]).then_inc(s_ldx.h, 16)
            E.add("sync", ld_init)

            n_wld1 = s_wld.reserve(16 * 3 * NCH)

            def ld_w(g):
                for w_d, w_s in ((wqT, wq_s), (wkT, wk_s), (wvT, wv_s)):
                    w_r = w_d.rearrange("(c p) o -> c p o", p=128)
                    for c in range(NCH):
                        g.dma_start(out=w_s[:, c, :], in_=w_r[c]).then_inc(s_wld.h, 16)
            E.add("gpsimd", ld_w)

            n_cst = s_cst.reserve()

            def consts(v):
                v.memset(onecol[:, :], 1.0)
                v.memset(onerow[:, :], 1.0)
                v.memset(cexp[:, :], -CEXP).then_inc(s_cst.h, 1)
                v.wait_ge(s_ldx.h, n_ldi1)
                v.wait_ge(s_wld.h, n_wld1)
            E.add("vector", consts)

            def eng_gate1(e, a=n_cst, b_=n_ldi1, c_=n_wld1):
                e.wait_ge(s_cst.h, a)
                e.wait_ge(s_ldx.h, b_)
                e.wait_ge(s_wld.h, c_)
            E.add("scalar", eng_gate1)
            E.add("tensor", eng_gate1)

            x_r = x_in.rearrange("(g p) d -> g p d", p=128)
            ln2_mark = []
            tp_mark = []
            for s_i in range(NSB):
                for tt in range(4):
                    g = 4 * s_i + tt
                    t_x = s_x[g % 2].reserve(16)
                    need_gx = ln2_mark[g - 2] if g >= 2 else 0

                    def ld_x(sync, g=g, need=need_gx):
                        if need > 0:
                            sync.wait_ge(s_ln2.h, need)
                        sync.dma_start(out=xs[:, g % 2, :], in_=x_r[g]).then_inc(
                            s_x[g % 2].h, 16)
                    E.add("sync", ld_x)

                    need_tp = tp_mark[g - 1] if g >= 1 else 0

                    def v_pre(v, g=g, t_x=t_x, need_tp=need_tp):
                        v.wait_ge(s_x[g % 2].h, t_x)
                        if need_tp > 0:
                            v.wait_ge(s_tp.h, need_tp)
                    E.add("vector", v_pre)
                    queue_ln(E, (lambda g=g: xs[:, g % 2, :]), (lambda: lnb[:, :]))
                    ln2_mark.append(s_ln2.n)

                    for rnd in range(2):
                        need_ln = s_ln2.n
                        need_ev = s_evT.n
                        t_tpv = s_tp.reserve()

                        def t_tp(t, rnd=rnd, need_ln=need_ln, need_ev=need_ev):
                            if rnd == 0:
                                t.wait_ge(s_ln2.h, need_ln)
                            if need_ev > 0:
                                t.wait_ge(s_evT.h, need_ev)
                            for i in range(8):
                                c = rnd * 8 + i
                                mmi = t.transpose(pT[:, i * 128:(i + 1) * 128],
                                                  lnb[:, c * 128:(c + 1) * 128], ids[:, :])
                            mmi.then_inc(s_tp.h, 1)
                        E.add("tensor", t_tp)

                        s_evT.reserve()

                        def v_ev(v, tt=tt, rnd=rnd, need=t_tpv):
                            v.wait_ge(s_tp.h, need)
                            v.tensor_copy(
                                lnT[:, rnd * 8:(rnd + 1) * 8, tt * 128:(tt + 1) * 128],
                                pT[:, :].rearrange("p (c f) -> p c f", f=128)).then_inc(
                                s_evT.h, 1)
                        E.add("vector", v_ev)
                    tp_mark.append(s_tp.n)

                ev_all = s_evT.n

                def t_sb_gate(t, need=ev_all):
                    t.wait_ge(s_evT.h, need)
                E.add("tensor", t_sb_gate)

                for h in range(HL):
                    for which, w_s in (("q", wq_s), ("k", wk_s)):
                        need_pe = s_pe.n
                        t_prj = s_proj.reserve()

                        def t_proj(t, h=h, w_s=w_s, need_pe=need_pe):
                            if need_pe > 0:
                                t.wait_ge(s_pe.h, need_pe)
                            for c in range(NCH):
                                mmi = t.matmul(pQK[:, :], w_s[:, c, h * 128:(h + 1) * 128],
                                               lnT[:, c, :], start=(c == 0),
                                               stop=(c == NCH - 1))
                            mmi.then_inc(s_proj.h, 1)
                        E.add("tensor", t_proj)

                        need_rope = s_rope.n
                        t_pev = s_pe.reserve()

                        def v_pe(v, need=t_prj, need_rope=need_rope):
                            v.wait_ge(s_proj.h, need)
                            if need_rope > 0:
                                v.wait_ge(s_rope.h, need_rope)
                            v.tensor_copy(pre[:, :], pQK[:, :]).then_inc(s_pe.h, 1)
                        E.add("vector", v_pe)

                        need_rp = s_rope.n
                        t_swv = s_sw.reserve()

                        def t_sw(t, need=t_pev, need_rp=need_rp):
                            t.wait_ge(s_pe.h, need)
                            if need_rp > 0:
                                t.wait_ge(s_rope.h, need_rp)
                            t.matmul(pSw[:, :], psw[:, :], pre[:, :],
                                     start=True, stop=True).then_inc(s_sw.h, 1)
                        E.add("tensor", t_sw)

                        ridx = s_rope.n
                        rslot = ridx % 2
                        va = s_v.reserve(2)
                        need_wr = s_spq[rslot].n if ridx >= 2 else 0
                        t_rp = s_rope.reserve()

                        def v_rope(v, s_i=s_i, rslot=rslot, need=t_swv, va=va,
                                   need_wr=need_wr):
                            v.wait_ge(s_sw.h, need)
                            v.tensor_copy(swp[:, :], pSw[:, :])
                            v.tensor_tensor(tmpA[:, :], pre[:, :],
                                            ccs[:, s_i * SBK:(s_i + 1) * SBK],
                                            op=ALU.mult).then_inc(s_v.h, 1)
                            v.wait_ge(s_v.h, va - 1)
                            v.tensor_tensor(swp[:, :], swp[:, :],
                                            ss2s[:, s_i * SBK:(s_i + 1) * SBK],
                                            op=ALU.mult).then_inc(s_v.h, 1)
                            v.wait_ge(s_v.h, va)
                            if need_wr > 0:
                                v.wait_ge(s_spq[rslot].h, need_wr)
                            v.tensor_tensor(ropo[:, rslot, :], tmpA[:, :],
                                            swp[:, :], op=ALU.add).then_inc(s_rope.h, 1)
                        E.add("vector", v_rope)

                        s_spq[rslot].reserve(16)

                        def sc_spill(scl, s_i=s_i, h=h, which=which, rslot=rslot,
                                     need=t_rp):
                            scl.wait_ge(s_rope.h, need)
                            if which == "q":
                                dst = qsp[h][:, s_i, :]
                            else:
                                dst = ksp[h][:, 4 * s_i:4 * s_i + 4, :].rearrange(
                                    "p a b -> p (a b)")
                            scl.dma_start(out=dst, in_=ropo[:, rslot, :]).then_inc(
                                s_spq[rslot].h, 16)
                        E.add("scalar", sc_spill)

                for tt in range(4):
                    g = 4 * s_i + tt
                    need_vev = s_vev.n
                    t_vp = s_proj.reserve()

                    def t_v(t, tt=tt, need_vev=need_vev):
                        if need_vev > 0:
                            t.wait_ge(s_vev.h, need_vev)
                        for c in range(NCH):
                            mmi = t.matmul(pV[:, :], lnT[:, c, tt * 128:(tt + 1) * 128],
                                           wv_s[:, c, :], start=(c == 0), stop=(c == NCH - 1))
                        mmi.then_inc(s_proj.h, 1)
                    E.add("tensor", t_v)

                    vslot = s_vev.n % 2
                    need_wr = s_spv[vslot].n if g >= 2 else 0
                    t_vev = s_vev.reserve()

                    def v_vev(v, vslot=vslot, need=t_vp, need_wr=need_wr):
                        v.wait_ge(s_proj.h, need)
                        if need_wr > 0:
                            v.wait_ge(s_spv[vslot].h, need_wr)
                        v.tensor_copy(vstg[:, vslot, :], pV[:, :]).then_inc(s_vev.h, 1)
                    E.add("vector", v_vev)

                    s_spv[vslot].reserve(16 * HL)

                    def sc_vsp(scl, g=g, vslot=vslot, need=t_vev):
                        scl.wait_ge(s_vev.h, need)
                        for h in range(HL):
                            scl.dma_start(
                                out=vsp[h][:, g, :],
                                in_=vstg[:, vslot, h * 128:(h + 1) * 128]).then_inc(
                                s_spv[vslot].h, 16)
                    E.add("scalar", sc_vsp)

            def p1_fin(scl, a0=s_spq[0].n, a1=s_spq[1].n, b0=s_spv[0].n, b1=s_spv[1].n):
                scl.wait_ge(s_spq[0].h, a0)
                scl.wait_ge(s_spq[1].h, a1)
                scl.wait_ge(s_spv[0].h, b0)
                scl.wait_ge(s_spv[1].h, b1)
            E.add("scalar", p1_fin)
            E.flush(blk1)

        with contextlib.ExitStack() as es2:
            sb2 = lambda n_, sh, dt: es2.enter_context(nc.sbuf_tensor(n_, sh, dt))
            ps2 = lambda n_, sh, dt: es2.enter_context(nc.psum_tensor(n_, sh, dt))
            msk = sb2("msk", [128, 4, 512], BF16)
            qbuf = sb2("qbuf", [128, 2, 512], BF16)
            kbuf = sb2("kbuf", [128, 2, CKT * 128], BF16)
            vbuf = sb2("vbuf", [128, 2, CKT * 128], BF16)
            pb = sb2("pb", [128, 2, 512], BF16)
            zinv = sb2("zinv", [1, 512], F32)
            binv = sb2("binv", [128, 512], F32)
            oout = sb2("oout", [128, 2, 512], BF16)
            pS0 = ps2("pS0", [128, 512], F32)
            pS1 = ps2("pS1", [128, 512], F32)
            pO = ps2("pO", [128, 512], F32)
            pZ = ps2("pZ", [128, 512], F32)
            pB = ps2("pB", [128, 512], F32)
            pSb = [pS0, pS1]

            blk2 = es2.enter_context(nc.Block("p2"))
            E = Emit()

            n_mskld = s_ldx.reserve(16 * 4)

            def ld_msk(sync):
                m_r = masks_in.rearrange("(t p) q -> t p q", p=128)
                for i in range(4):
                    sync.dma_start(out=msk[:, i, :], in_=m_r[i]).then_inc(s_ldx.h, 16)
            E.add("sync", ld_msk)

            def eng_gate2(e, need=n_mskld):
                e.wait_ge(s_ldx.h, need)
            E.add("vector", eng_gate2)
            E.add("tensor", eng_gate2)
            E.add("scalar", eng_gate2)

            mm_mark_iter = []
            kcons_mark = []
            gchunk = 0
            for qb in range(NSB):
                nkt = 4 * (qb + 1)
                nchunks = (nkt + CKT - 1) // CKT
                for h in range(HL):
                    it = qb * HL + h
                    qslot = it % 2
                    need_qg = mm_mark_iter[it - 2] if it >= 2 else 0
                    q_mark = s_q[qslot].reserve(16)

                    def ld_q(sync, qb=qb, h=h, qslot=qslot, need=need_qg):
                        if need > 0:
                            sync.wait_ge(s_mm.h, need)
                        sync.dma_start(out=qbuf[:, qslot, :],
                                       in_=qsp[h][:, qb, :]).then_inc(s_q[qslot].h, 16)
                    E.add("sync", ld_q)

                    ch_base = gchunk
                    ld_marks = {}

                    def emit_ld_kv(ch, qb=qb, h=h, nkt=nkt, ch_base=ch_base,
                                   ld_marks=ld_marks):
                        nonlocal gchunk
                        k0 = ch * CKT
                        nk = min(CKT, nkt - k0)
                        gc = ch_base + ch
                        cs = gc % 2
                        need = kcons_mark[gc - 2] if gc >= 2 else 0
                        ld_marks[ch] = s_kv[cs].reserve(32)

                        def ld_kv(sync, h=h, k0=k0, nk=nk, cs=cs, need=need):
                            if need > 0:
                                sync.wait_ge(s_av.h, need)
                            sync.dma_start(
                                out=kbuf[:, cs, 0:nk * 128],
                                in_=ksp[h][:, k0:k0 + nk, :].rearrange(
                                    "p a b -> p (a b)")).then_inc(s_kv[cs].h, 16)
                            sync.dma_start(
                                out=vbuf[:, cs, 0:nk * 128],
                                in_=vsp[h][:, k0:k0 + nk, :].rearrange(
                                    "p a b -> p (a b)")).then_inc(s_kv[cs].h, 16)
                        E.add("sync", ld_kv)
                        gchunk += 1

                    emit_ld_kv(0)
                    if nchunks > 1:
                        emit_ld_kv(1)

                    o_guard = s_o.n

                    for kt in range(nkt):
                        ch = kt // CKT
                        cs = (ch_base + ch) % 2
                        ko = kt - ch * CKT
                        sslot = kt % 2
                        diag = kt >= 4 * qb
                        lastk = (ko == CKT - 1 or kt == nkt - 1)

                        need_exp = s_exp.n - 1
                        t_sv = s_mm.reserve()

                        def t_s(t, kt=kt, cs=cs, ko=ko, sslot=sslot, qslot=qslot,
                                need_ld=ld_marks[ch], need_exp=need_exp,
                                o_guard=o_guard, q_mark=q_mark):
                            if kt == 0:
                                if o_guard > 0:
                                    t.wait_ge(s_o.h, o_guard)
                                t.wait_ge(s_q[qslot].h, q_mark)
                                t.wait_ge(s_kv[cs].h, need_ld)
                            elif ko == 0:
                                t.wait_ge(s_kv[cs].h, need_ld)
                            if need_exp > 0:
                                t.wait_ge(s_exp.h, need_exp)
                            t.matmul(pSb[sslot][:, :],
                                     kbuf[:, cs, ko * 128:(ko + 1) * 128],
                                     qbuf[:, qslot, :], start=True,
                                     stop=True).then_inc(s_mm.h, 1)
                        E.add("tensor", t_s)

                        need_av = s_av.n - 1
                        t_ev = s_exp.reserve()

                        def s_e(scl, sslot=sslot, need_mm=t_sv, need_av=need_av):
                            scl.wait_ge(s_mm.h, need_mm)
                            if need_av > 0:
                                scl.wait_ge(s_av.h, need_av)
                            scl.activation(out=pb[:, sslot, :], in_=pSb[sslot][:, :],
                                           func=AF.Exp, bias=cexp[:, :],
                                           scale=SCL).then_inc(s_exp.h, 1)
                        E.add("scalar", s_e)

                        if diag:
                            t_mk = s_msk.reserve()

                            def v_m(v, qb=qb, kt=kt, sslot=sslot, need=t_ev):
                                v.wait_ge(s_exp.h, need)
                                v.tensor_tensor(pb[:, sslot, :], pb[:, sslot, :],
                                                msk[:, kt - 4 * qb, :],
                                                op=ALU.mult).then_inc(s_msk.h, 1)
                            E.add("vector", v_m)

                        need_m = s_msk.n
                        t_avv = s_av.reserve()

                        def t_zav(t, kt=kt, cs=cs, ko=ko, sslot=sslot, diag=diag,
                                  need_e=t_ev, need_m=need_m, nkt=nkt):
                            if diag:
                                t.wait_ge(s_msk.h, need_m)
                            else:
                                t.wait_ge(s_exp.h, need_e)
                            t.matmul(pZ[0:1, :], onecol[:, :], pb[:, sslot, :],
                                     start=(kt == 0), stop=(kt == nkt - 1))
                            t.matmul(pO[:, :], vbuf[:, cs, ko * 128:(ko + 1) * 128],
                                     pb[:, sslot, :], start=(kt == 0),
                                     stop=(kt == nkt - 1)).then_inc(s_av.h, 1)
                        E.add("tensor", t_zav)
                        if lastk:
                            kcons_mark.append(s_av.n)
                            if ch + 2 < nchunks:
                                emit_ld_kv(ch + 2)

                    mm_mark_iter.append(s_mm.n)

                    t_ziv = s_zi.reserve()

                    def v_zi(v, need=s_av.n):
                        v.wait_ge(s_av.h, need)
                        v.reciprocal(zinv[:, :], pZ[0:1, :]).then_inc(s_zi.h, 1)
                    E.add("vector", v_zi)

                    t_bcv = s_bc.reserve()

                    def t_bc(t, need=t_ziv):
                        t.wait_ge(s_zi.h, need)
                        t.matmul(pB[:, :], onerow[:, :], zinv[:, :],
                                 start=True, stop=True).then_inc(s_bc.h, 1)
                    E.add("tensor", t_bc)

                    oslot = it % 2
                    need_st = s_sto[oslot].n if it >= 2 else 0
                    va = s_v.reserve()
                    t_ov = s_o.reserve()

                    def v_o(v, oslot=oslot, need_bc=t_bcv, need_st=need_st, va=va):
                        v.wait_ge(s_bc.h, need_bc)
                        v.tensor_copy(binv[:, :], pB[:, :]).then_inc(s_v.h, 1)
                        if need_st > 0:
                            v.wait_ge(s_sto[oslot].h, need_st)
                        v.wait_ge(s_v.h, va)
                        v.tensor_tensor(oout[:, oslot, :], pO[:, :], binv[:, :],
                                        op=ALU.mult).then_inc(s_o.h, 1)
                    E.add("vector", v_o)

                    s_sto[oslot].reserve(16)

                    def sc_o(scl, qb=qb, h=h, oslot=oslot, need=t_ov):
                        scl.wait_ge(s_o.h, need)
                        k = qb // 4
                        scl.dma_start(
                            out=ccg_in[k][h * 128:(h + 1) * 128,
                                          (qb % 4) * 512:(qb % 4 + 1) * 512],
                            in_=oout[:, oslot, :]).then_inc(s_sto[oslot].h, 16)
                    E.add("scalar", sc_o)

                if qb % 4 == 3:
                    s_cc.reserve()

                    def g_cc(g, k=qb // 4, need0=s_sto[0].n, need1=s_sto[1].n):
                        g.wait_ge(s_sto[0].h, need0)
                        g.wait_ge(s_sto[1].h, need1)
                        g.collective_compute(
                            "AllGather", ALU.bypass,
                            replica_groups=[[0, 1, 2, 3], [4, 5, 6, 7]],
                            ins=[ccg_in[k].ap().opt()],
                            outs=[ccg_out[k].ap().opt()]).then_inc(s_cc.h, 1)
                    E.add("gpsimd", g_cc)
            E.flush(blk2)

        with contextlib.ExitStack() as es3:
            sb3 = lambda n_, sh, dt: es3.enter_context(nc.sbuf_tensor(n_, sh, dt))
            ps3 = lambda n_, sh, dt: es3.enter_context(nc.psum_tensor(n_, sh, dt))
            wo_s = sb3("wo_s", [128, NCH, D], BF16)
            aT = sb3("aT", [128, NCH, 512], BF16)
            y1T = sb3("y1T", [128, NCH, 512], BF16)
            y1 = sb3("y1", [128, 4, D], BF16)
            xqb = sb3("xqb", [128, 2, D], F32)
            t3 = sb3("t3", [128, D], F32)
            r1b = sb3("r1b", [128, 2, D], F32)
            pY = ps3("pY", [128, 512], F32)
            pT3 = ps3("pT3", [128, 1024], BF16)

            blk3 = es3.enter_context(nc.Block("p3"))
            E = Emit()

            n_wld3 = s_wld.reserve(16 * NCH)

            def ld_wo(g):
                w_r = woT.rearrange("(c p) o -> c p o", p=128)
                for c in range(NCH):
                    g.dma_start(out=wo_s[:, c, :], in_=w_r[c]).then_inc(s_wld.h, 16)
            E.add("gpsimd", ld_wo)

            def t_gate3(t, need=n_wld3):
                t.wait_ge(s_wld.h, need)
            E.add("tensor", t_gate3)

            xq_r = xq_in.rearrange("(g p) d -> g p d", p=128)
            r1_r = r1sp.rearrange("(g p) d -> g p d", p=128)

            r1_mark = []
            mm3_mark = []
            for tb in range(NBLK):
                E.flush(blk3)
                need_mm = mm3_mark[tb - 1] if tb >= 1 else 0
                lda_tb = s_lda.reserve(16 * NCH)

                def sync_aT(sync, tb=tb, need_mm=need_mm):
                    if need_mm > 0:
                        sync.wait_ge(s_mm3.h, need_mm)
                    pid = sync.partition_id()
                    for core in nc.Switch(engines=[sync], index=[pid], n=8):
                        tq = core % 4
                        sync.wait_ge(s_cc.h, tq + 1)
                        co = ccg_out[tq]
                        for cb in range(NCH):
                            sync.dma_start(
                                out=aT[:, cb, :],
                                in_=co[cb * 128:(cb + 1) * 128,
                                       tb * 512:(tb + 1) * 512]).then_inc(s_lda.h, 16)
                blk3.sync(sync_aT)

                for oc in range(NCH):
                    need_ev = s_yev.n
                    t_mmv = s_mm3.reserve()

                    def t_op(t, oc=oc, need_ld=lda_tb, need_ev=need_ev):
                        if oc == 0:
                            t.wait_ge(s_lda.h, need_ld)
                        if need_ev > 0:
                            t.wait_ge(s_yev.h, need_ev)
                        for c in range(NCH):
                            mmi = t.matmul(pY[:, :], wo_s[:, c, oc * 128:(oc + 1) * 128],
                                           aT[:, c, :], start=(c == 0), stop=(c == NCH - 1))
                        mmi.then_inc(s_mm3.h, 1)
                    E.add("tensor", t_op)

                    s_yev.reserve()

                    def v_ye(v, oc=oc, need=t_mmv):
                        v.wait_ge(s_mm3.h, need)
                        v.tensor_copy(y1T[:, oc, :], pY[:, :]).then_inc(s_yev.h, 1)
                    E.add("vector", v_ye)
                mm3_mark.append(s_mm3.n)

                for rnd in range(8):
                    need_ev = s_yev.n
                    need_pt = s_evT.n
                    t_tpv = s_tp.reserve()

                    def t_tp3(t, rnd=rnd, need_ev=need_ev, need_pt=need_pt):
                        if rnd == 0:
                            t.wait_ge(s_yev.h, need_ev)
                        if need_pt > 0:
                            t.wait_ge(s_evT.h, need_pt)
                        for i in range(8):
                            j = rnd * 8 + i
                            oc, sub = j // 4, j % 4
                            mmi = t.transpose(pT3[:, i * 128:(i + 1) * 128],
                                              y1T[:, oc, sub * 128:(sub + 1) * 128],
                                              ids[:, :])
                        mmi.then_inc(s_tp.h, 1)
                    E.add("tensor", t_tp3)

                    s_evT.reserve()

                    def v_pt3(v, rnd=rnd, need=t_tpv):
                        v.wait_ge(s_tp.h, need)
                        oc0 = rnd * 2
                        v.tensor_copy(
                            y1[:, :, oc0 * 128:(oc0 + 2) * 128].rearrange(
                                "p s (c f) -> p c s f", f=128),
                            pT3[:, :].rearrange("p (c s f) -> p c s f",
                                                c=2, s=4)).then_inc(s_evT.h, 1)
                    E.add("vector", v_pt3)

                for sub in range(4):
                    g = tb * 4 + sub
                    need_r1g = r1_mark[g - 2] if g >= 2 else 0
                    t_xqv = s_xq[g % 2].reserve(16)

                    def ld_xq(sync, g=g, need=need_r1g):
                        if need > 0:
                            sync.wait_ge(s_r1.h, need)
                        sync.dma_start(out=xqb[:, g % 2, :], in_=xq_r[g]).then_inc(
                            s_xq[g % 2].h, 16)
                    E.add("sync", ld_xq)

                    need_t = s_evT.n

                    def v_lngate(v, need_t=need_t):
                        v.wait_ge(s_evT.h, need_t)
                    E.add("vector", v_lngate)
                    tnorm = queue_ln(E, (lambda sub=sub: y1[:, sub, :]),
                                     (lambda: t3[:, :]))

                    need_wr = s_sr[g % 2].n if g >= 2 else 0
                    t_r1v = s_r1.reserve()

                    def v_r1(v, g=g, need_ld=t_xqv, need_wr=need_wr, tnorm=tnorm):
                        v.wait_ge(s_xq[g % 2].h, need_ld)
                        v.wait_ge(s_ln2.h, tnorm)
                        if need_wr > 0:
                            v.wait_ge(s_sr[g % 2].h, need_wr)
                        v.tensor_tensor(r1b[:, g % 2, :], t3[:, :],
                                        xqb[:, g % 2, :], op=ALU.add).then_inc(s_r1.h, 1)
                    E.add("vector", v_r1)
                    r1_mark.append(s_r1.n)

                    s_sr[g % 2].reserve(16)

                    def sc_r1(scl, g=g, need=t_r1v):
                        scl.wait_ge(s_r1.h, need)
                        scl.dma_start(out=r1_r[g], in_=r1b[:, g % 2, :]).then_inc(
                            s_sr[g % 2].h, 16)
                    E.add("scalar", sc_r1)

            def p3_fin(scl, a0=s_sr[0].n, a1=s_sr[1].n):
                scl.wait_ge(s_sr[0].h, a0)
                scl.wait_ge(s_sr[1].h, a1)
            E.add("scalar", p3_fin)
            E.flush(blk3)

        with contextlib.ExitStack() as es4:
            sb4 = lambda n_, sh, dt: es4.enter_context(nc.sbuf_tensor(n_, sh, dt))
            ps4 = lambda n_, sh, dt: es4.enter_context(nc.psum_tensor(n_, sh, dt))
            wb_s = sb4("wb_s", [128, NCH, D], BF16)
            awr_s = sb4("awr_s", [128, NCH, 72], BF16)
            bfl_s = sb4("bfl_s", [64, D], BF16)
            bb_s = sb4("bb_s", [128, 16], F32)
            r1a = sb4("r1a", [128, 2, D], F32)
            r1c = sb4("r1c", [128, 2, D], F32)
            ln2 = sb4("ln2", [128, D], BF16)
            ln2T = sb4("ln2T", [128, NCH, 512], BF16)
            hbuf = sb4("hbuf", [128, 512], BF16)
            hbufL = sb4("hbufL", [8, 512], F32)
            lgt = sb4("lgt", [128, 4, 8], F32)
            negm1 = sb4("negm1", [128, 1], F32)
            m2b = sb4("m2b", [128, 1], F32)
            eqb = sb4("eqb", [128, 8], F32)
            lmk = sb4("lmk", [128, 8], F32)
            pexp = sb4("pexp", [128, 8], F32)
            indb = sb4("indb", [128, 8], F32)
            wgt = sb4("wgt", [128, 8], F32)
            zg = sb4("zg", [128, 1], F32)
            zgi = sb4("zgi", [128, 1], F32)
            wn = sb4("wn", [128, 8], F32)
            wnT = sb4("wnT", [8, 128], BF16)
            wexpT = sb4("wexpT", [64, 512], BF16)
            h2 = sb4("h2", [64, 512], BF16)
            ffnT = sb4("ffnT", [128, NCH, 512], BF16)
            ffn = sb4("ffn", [128, 4, D], BF16)
            t4 = sb4("t4", [128, D], F32)
            r2b = sb4("r2b", [128, 2, D], F32)
            pY4 = ps4("pY4", [128, 512], F32)
            pT4 = ps4("pT4", [128, 1024], BF16)
            pH = ps4("pH", [128, 512], F32)
            pG = ps4("pG", [128, 512], F32)

            blk4 = es4.enter_context(nc.Block("p4"))
            E = Emit()

            n_wld4 = s_wld.reserve(16 * (2 * NCH + 1))

            def ld_w4(g):
                w_r = wbT.rearrange("(c p) o -> c p o", p=128)
                for c in range(NCH):
                    g.dma_start(out=wb_s[:, c, :], in_=w_r[c]).then_inc(s_wld.h, 16)
                a_r = awrT.rearrange("(c p) o -> c p o", p=128)
                for c in range(NCH):
                    g.dma_start(out=awr_s[:, c, :], in_=a_r[c]).then_inc(s_wld.h, 16)
                g.dma_start(out=bfl_s[:, :], in_=bflT[:, :]).then_inc(s_wld.h, 16)
            E.add("gpsimd", ld_w4)

            n_ld4 = s_ldx.reserve(16)

            def ld_bb(sync):
                sync.dma_start(out=bb_s[:, :], in_=bbase[:, :]).then_inc(s_ldx.h, 16)
            E.add("sync", ld_bb)

            def gate4(e, a=n_wld4, b_=n_ld4):
                e.wait_ge(s_wld.h, a)
                e.wait_ge(s_ldx.h, b_)
            E.add("tensor", gate4)
            E.add("vector", gate4)
            E.add("scalar", gate4)

            out_r = out_q.rearrange("(g p) d -> g p d", p=128)

            ln3_mark = []
            r2_mark = []
            for tb in range(NBLK):
                for sub in range(4):
                    g = tb * 4 + sub
                    need_lg = ln3_mark[g - 2] if g >= 2 else 0
                    t_rav = s_ra[g % 2].reserve(16)

                    def ld_r1(sync, g=g, need=need_lg):
                        if need > 0:
                            sync.wait_ge(s_ln2.h, need)
                        sync.dma_start(out=r1a[:, g % 2, :], in_=r1_r[g]).then_inc(
                            s_ra[g % 2].h, 16)
                    E.add("sync", ld_r1)

                    need_tp = s_tp.n

                    def v_lngate4(v, g=g, t_rav=t_rav, need_tp=need_tp):
                        v.wait_ge(s_ra[g % 2].h, t_rav)
                        if need_tp > 0:
                            v.wait_ge(s_tp.h, need_tp)
                    E.add("vector", v_lngate4)
                    queue_ln(E, (lambda g=g: r1a[:, g % 2, :]), (lambda: ln2[:, :]))
                    ln3_mark.append(s_ln2.n)

                    for rnd in range(2):
                        need_ln = s_ln2.n
                        need_ev = s_evT.n
                        t_tpv = s_tp.reserve()

                        def t_tp4(t, rnd=rnd, need_ln=need_ln, need_ev=need_ev):
                            if rnd == 0:
                                t.wait_ge(s_ln2.h, need_ln)
                            if need_ev > 0:
                                t.wait_ge(s_evT.h, need_ev)
                            for i in range(8):
                                c = rnd * 8 + i
                                mmi = t.transpose(pT4[:, i * 128:(i + 1) * 128],
                                                  ln2[:, c * 128:(c + 1) * 128], ids[:, :])
                            mmi.then_inc(s_tp.h, 1)
                        E.add("tensor", t_tp4)

                        s_evT.reserve()

                        def v_ev4(v, sub=sub, rnd=rnd, need=t_tpv):
                            v.wait_ge(s_tp.h, need)
                            v.tensor_copy(
                                ln2T[:, rnd * 8:(rnd + 1) * 8, sub * 128:(sub + 1) * 128],
                                pT4[:, :].rearrange("p (c f) -> p c f", f=128)).then_inc(
                                s_evT.h, 1)
                        E.add("vector", v_ev4)

                need_g = s_g.n
                t_awrv = s_mm3.reserve()

                def t_awr(t, need=s_evT.n, need_g=need_g):
                    t.wait_ge(s_evT.h, need)
                    if need_g > 0:
                        t.wait_ge(s_g.h, need_g)
                    for c in range(NCH):
                        mmi = t.matmul(pH[0:72, :], awr_s[:, c, :], ln2T[:, c, :],
                                       start=(c == 0), stop=(c == NCH - 1))
                    mmi.then_inc(s_mm3.h, 1)
                E.add("tensor", t_awr)

                va = s_v.reserve()
                t_gv = s_g.reserve()

                def v_hev(v, need=t_awrv, va=va):
                    v.wait_ge(s_mm3.h, need)
                    v.tensor_copy(hbuf[0:64, :], pH[0:64, :]).then_inc(s_v.h, 1)
                    v.wait_ge(s_v.h, va)
                    v.tensor_copy(hbufL[:, :], pH[64:72, :]).then_inc(s_g.h, 1)
                E.add("vector", v_hev)

                need_cons = s_lda.n
                for sub in range(4):
                    t_lgv = s_tp.reserve()

                    def t_lg(t, sub=sub, need=t_gv, need_cons=need_cons):
                        if sub == 0:
                            t.wait_ge(s_g.h, need)
                            if need_cons > 0:
                                t.wait_ge(s_lda.h, need_cons)
                        t.transpose(pG[:, sub * 8:(sub + 1) * 8],
                                    hbufL[:, sub * 128:(sub + 1) * 128],
                                    idsf[0:8, 0:8]).then_inc(s_tp.h, 1)
                    E.add("tensor", t_lg)

                lg_need = s_tp.n
                t_lgc = s_lda.reserve()

                def v_lg(v, need=lg_need):
                    v.wait_ge(s_tp.h, need)
                    v.tensor_copy(
                        lgt[:, :, :], pG[:, 0:32].rearrange(
                            "p (s e) -> p s e", e=8)).then_inc(s_lda.h, 1)
                E.add("vector", v_lg)

                for sub in range(4):
                    va = s_v.reserve(4)
                    t_ind = s_msk.reserve()

                    def v_g1(v, sub=sub, need=t_lgc, va=va):
                        if sub == 0:
                            v.wait_ge(s_lda.h, need)
                        l = lgt[:, sub, :]
                        v.reduce_max(negm1[:, :], l, axis=AX.X,
                                     negate=True).then_inc(s_v.h, 1)
                        v.wait_ge(s_v.h, va - 3)
                        v.tensor_scalar(out=eqb[:, :], in0=l, scalar1=negm1[:, :],
                                        scalar2=0.0, op0=ALU.add,
                                        op1=ALU.is_ge).then_inc(s_v.h, 1)
                        v.wait_ge(s_v.h, va - 2)
                        v.scalar_tensor_tensor(out=lmk[:, :], in0=eqb[:, :],
                                               scalar=-30000.0, in1=l,
                                               op0=ALU.mult, op1=ALU.add).then_inc(s_v.h, 1)
                        v.wait_ge(s_v.h, va - 1)
                        v.reduce_max(m2b[:, :], lmk[:, :], axis=AX.X).then_inc(s_v.h, 1)
                        v.wait_ge(s_v.h, va)
                        v.tensor_scalar(out=indb[:, :], in0=l, scalar1=m2b[:, :],
                                        scalar2=None, op0=ALU.is_ge).then_inc(s_msk.h, 1)
                    E.add("vector", v_g1)

                    t_px = s_rs.reserve()

                    def s_px(scl, sub=sub, need=t_ind):
                        scl.wait_ge(s_msk.h, need)
                        scl.activation(out=pexp[:, :], in_=lgt[:, sub, :],
                                       func=AF.Exp, bias=negm1[:, :],
                                       scale=1.0).then_inc(s_rs.h, 1)
                    E.add("scalar", s_px)

                    va = s_v.reserve(3)
                    t_wnv = s_ln2.reserve()

                    def v_g2(v, need=t_px, va=va):
                        v.wait_ge(s_rs.h, need)
                        v.tensor_tensor(wgt[:, :], indb[:, :], pexp[:, :],
                                        op=ALU.mult).then_inc(s_v.h, 1)
                        v.wait_ge(s_v.h, va - 2)
                        v.reduce_sum(zg[:, :], wgt[:, :], axis=AX.X).then_inc(s_v.h, 1)
                        v.wait_ge(s_v.h, va - 1)
                        v.reciprocal(zgi[:, :], zg[:, :]).then_inc(s_v.h, 1)
                        v.wait_ge(s_v.h, va)
                        v.tensor_scalar(out=wn[:, :], in0=wgt[:, :],
                                        scalar1=zgi[:, :], scalar2=LORA_SCALE,
                                        op0=ALU.mult, op1=ALU.mult).then_inc(s_ln2.h, 1)
                    E.add("vector", v_g2)

                    need_ev = s_yev.n
                    t_wtv = s_tp.reserve()

                    def t_wnT(t, need=t_wnv, need_ev=need_ev):
                        t.wait_ge(s_ln2.h, need)
                        if need_ev > 0:
                            t.wait_ge(s_yev.h, need_ev)
                        t.transpose(pG[0:8, 128:256], wn[:, :],
                                    idsf[:, :]).then_inc(s_tp.h, 1)
                    E.add("tensor", t_wnT)

                    t_wcv = s_r1.reserve()

                    def v_wnT(v, need=t_wtv):
                        v.wait_ge(s_tp.h, need)
                        v.tensor_copy(wnT[:, :], pG[0:8, 128:256]).then_inc(s_r1.h, 1)
                    E.add("vector", v_wnT)

                    t_e8v = s_bc.reserve()

                    def t_e8(t, need=t_wcv):
                        t.wait_ge(s_r1.h, need)
                        t.matmul(pG[0:64, 256:384], e8s[:, :], wnT[:, :],
                                 start=True, stop=True).then_inc(s_bc.h, 1)
                    E.add("tensor", t_e8)

                    s_yev.reserve()

                    def v_we(v, sub=sub, need=t_e8v):
                        v.wait_ge(s_bc.h, need)
                        v.tensor_copy(wexpT[:, sub * 128:(sub + 1) * 128],
                                      pG[0:64, 256:384]).then_inc(s_yev.h, 1)
                    E.add("vector", v_we)

                t_h2v = s_g.reserve()

                def v_h2(v, need=s_yev.n):
                    v.wait_ge(s_yev.h, need)
                    v.tensor_tensor(h2[:, :], hbuf[0:64, :], wexpT[:, :],
                                    op=ALU.mult).then_inc(s_g.h, 1)
                E.add("vector", v_h2)

                for oc in range(NCH):
                    need_ev = s_exp.n
                    t_fv = s_mm.reserve()

                    def t_ffn(t, oc=oc, need_h2=t_h2v, need_ev=need_ev):
                        if oc == 0:
                            t.wait_ge(s_g.h, need_h2)
                        if need_ev > 0:
                            t.wait_ge(s_exp.h, need_ev)
                        for c in range(NCH):
                            t.matmul(pY4[:, :], wb_s[:, c, oc * 128:(oc + 1) * 128],
                                     ln2T[:, c, :], start=(c == 0), stop=False)
                        t.matmul(pY4[:, :], bfl_s[:, oc * 128:(oc + 1) * 128],
                                 h2[:, :], start=False, stop=True).then_inc(s_mm.h, 1)
                    E.add("tensor", t_ffn)

                    s_exp.reserve()

                    def s_fev(scl, oc=oc, need=t_fv):
                        scl.wait_ge(s_mm.h, need)
                        scl.activation(out=ffnT[:, oc, :], in_=pY4[:, :],
                                       func=AF.Identity, bias=bb_s[:, oc:oc + 1],
                                       scale=1.0).then_inc(s_exp.h, 1)
                    E.add("scalar", s_fev)

                for rnd in range(8):
                    need_ev = s_exp.n
                    need_pt = s_evT.n
                    t_tpv = s_tp.reserve()

                    def t_tpf(t, rnd=rnd, need_ev=need_ev, need_pt=need_pt):
                        if rnd == 0:
                            t.wait_ge(s_exp.h, need_ev)
                        if need_pt > 0:
                            t.wait_ge(s_evT.h, need_pt)
                        for i in range(8):
                            j = rnd * 8 + i
                            oc, sub = j // 4, j % 4
                            mmi = t.transpose(pT4[:, i * 128:(i + 1) * 128],
                                              ffnT[:, oc, sub * 128:(sub + 1) * 128],
                                              ids[:, :])
                        mmi.then_inc(s_tp.h, 1)
                    E.add("tensor", t_tpf)

                    s_evT.reserve()

                    def v_ptf(v, rnd=rnd, need=t_tpv):
                        v.wait_ge(s_tp.h, need)
                        oc0 = rnd * 2
                        v.tensor_copy(
                            ffn[:, :, oc0 * 128:(oc0 + 2) * 128].rearrange(
                                "p s (c f) -> p c s f", f=128),
                            pT4[:, :].rearrange("p (c s f) -> p c s f",
                                                c=2, s=4)).then_inc(s_evT.h, 1)
                    E.add("vector", v_ptf)

                for sub in range(4):
                    g = tb * 4 + sub
                    need_r2g = r2_mark[g - 2] if g >= 2 else 0
                    t_rcv = s_rc[g % 2].reserve(16)

                    def ld_r1b(sync, g=g, need=need_r2g):
                        if need > 0:
                            sync.wait_ge(s_o.h, need)
                        sync.dma_start(out=r1c[:, g % 2, :], in_=r1_r[g]).then_inc(
                            s_rc[g % 2].h, 16)
                    E.add("sync", ld_r1b)

                    need_t = s_evT.n

                    def v_lngate5(v, need_t=need_t):
                        v.wait_ge(s_evT.h, need_t)
                    E.add("vector", v_lngate5)
                    tnorm = queue_ln(E, (lambda sub=sub: ffn[:, sub, :]),
                                     (lambda: t4[:, :]))

                    need_st = s_so[g % 2].n if g >= 2 else 0
                    t_r2v = s_o.reserve()

                    def v_r2(v, g=g, need_ld=t_rcv, need_st=need_st, tnorm=tnorm):
                        v.wait_ge(s_rc[g % 2].h, need_ld)
                        v.wait_ge(s_ln2.h, tnorm)
                        if need_st > 0:
                            v.wait_ge(s_so[g % 2].h, need_st)
                        v.tensor_tensor(r2b[:, g % 2, :], t4[:, :],
                                        r1c[:, g % 2, :], op=ALU.add).then_inc(s_o.h, 1)
                    E.add("vector", v_r2)
                    r2_mark.append(s_o.n)

                    s_so[g % 2].reserve(16)

                    def sc_out(scl, g=g, need=t_r2v):
                        scl.wait_ge(s_o.h, need)
                        scl.dma_start(out=out_r[g], in_=r2b[:, g % 2, :]).then_inc(
                            s_so[g % 2].h, 16)
                    E.add("scalar", sc_out)

            def p4_fin(scl, a0=s_so[0].n, a1=s_so[1].n):
                scl.wait_ge(s_so[0].h, a0)
                scl.wait_ge(s_so[1].h, a1)
            E.add("scalar", p4_fin)
            E.flush(blk4)

    return nc


def _host_tables():
    half = DK // 2
    inv_freq = 10000.0 ** (-2.0 * np.arange(half) / DK)
    ang = np.arange(S)[:, None] * inv_freq[None, :]
    cosT = np.cos(ang).T.astype(np.float32)
    sinT = np.sin(ang).T.astype(np.float32)
    cc = np.concatenate([cosT, cosT], axis=0).astype(ml_dtypes.bfloat16)
    ss2 = np.concatenate([-sinT, sinT], axis=0).astype(ml_dtypes.bfloat16)
    masks = np.zeros((4 * 128, 512), dtype=ml_dtypes.bfloat16)
    for i in range(4):
        for kk in range(128):
            masks[i * 128 + kk, i * 128 + kk:] = 1.0
    ident = np.eye(128, dtype=ml_dtypes.bfloat16)
    pswap = np.zeros((128, 128), dtype=ml_dtypes.bfloat16)
    for i in range(64):
        pswap[i + 64, i] = 1.0
        pswap[i, i + 64] = 1.0
    e8 = np.zeros((8, 64), dtype=ml_dtypes.bfloat16)
    for e in range(8):
        e8[e, e * 8:(e + 1) * 8] = 1.0
    return cc, ss2, masks, ident, pswap, e8


def _perm_rows(w):
    perm = []
    for h in range(H):
        base = h * DK
        perm += [base + 2 * i for i in range(DK // 2)]
        perm += [base + 2 * i + 1 for i in range(DK // 2)]
    return w[perm, :]


_NC_CACHE = {}


def kernel(L0_attn_x, L0_attn_wq, L0_attn_wk, L0_attn_wv, L0_attn_wo,
           L0_ffn_W_base, L0_ffn_b_base, L0_ffn_W_router, L0_ffn_A_w, L0_ffn_B_w):
    x = np.ascontiguousarray(np.asarray(L0_attn_x, dtype=np.float32))
    wq = np.asarray(L0_attn_wq, dtype=np.float32)
    wk = np.asarray(L0_attn_wk, dtype=np.float32)
    wv = np.asarray(L0_attn_wv, dtype=np.float32)
    wo = np.asarray(L0_attn_wo, dtype=np.float32)
    wb = np.asarray(L0_ffn_W_base, dtype=np.float32)
    bb = np.asarray(L0_ffn_b_base, dtype=np.float32)
    wr = np.asarray(L0_ffn_W_router, dtype=np.float32)
    aw = np.asarray(L0_ffn_A_w, dtype=np.float32)
    bw = np.asarray(L0_ffn_B_w, dtype=np.float32)

    cc, ss2, masks, ident, pswap, e8 = _host_tables()

    wqTf = np.ascontiguousarray(_perm_rows(wq).T)
    wkTf = np.ascontiguousarray(_perm_rows(wk).T)
    wvTf = np.ascontiguousarray(wv.T)
    woTf = np.ascontiguousarray(wo.T)
    wbTf = np.ascontiguousarray(wb.T)
    aflT = aw.reshape(64, D).T
    wrTf = wr.T
    awrTf = np.ascontiguousarray(np.concatenate([aflT, wrTf], axis=1))
    bflTf = np.ascontiguousarray(bw.transpose(0, 2, 1).reshape(64, D))
    bbasef = np.ascontiguousarray(bb.reshape(16, 128).T)

    if "nc" not in _NC_CACHE:
        _NC_CACHE["nc"] = build_program()
    nc = _NC_CACHE["nc"]

    in_maps = []
    for c in range(8):
        b = c // 4
        hg = c % 4
        tq = c % 4
        in_maps.append({
            "x": x[b],
            "xq": np.ascontiguousarray(x[b, tq * QT:(tq + 1) * QT, :]),
            "wqT": np.ascontiguousarray(wqTf[:, hg * 512:(hg + 1) * 512]),
            "wkT": np.ascontiguousarray(wkTf[:, hg * 512:(hg + 1) * 512]),
            "wvT": np.ascontiguousarray(wvTf[:, hg * 512:(hg + 1) * 512]),
            "woT": woTf, "wbT": wbTf, "awrT": awrTf, "bflT": bflTf, "bbase": bbasef,
            "ident": ident, "identf": np.eye(128, dtype=np.float32),
            "pswap": pswap, "e8": e8,
            "cctab": cc, "ss2tab": ss2, "masks": masks,
        })
    res = run_bass_kernel_spmd(nc, in_maps, core_ids=list(range(8)))
    out = np.empty((B, S, D), dtype=np.float32)
    for c in range(8):
        b = c // 4
        tq = c % 4
        out[b, tq * QT:(tq + 1) * QT, :] = res.results[c]["out_q"]
    return out


# version 03
# speedup vs baseline: 1.9882x; 1.5353x over previous
import contextlib
import numpy as np
import ml_dtypes
import concourse.bass as bass
import concourse.mybir as mybir
from concourse.bass_utils import run_bass_kernel_spmd

F32 = mybir.dt.float32
I32 = mybir.dt.int32
BF16 = mybir.dt.bfloat16
AF = mybir.ActivationFunctionType
ALU = mybir.AluOpType
AX = mybir.AxisListType

B = 2
S = 8192
D = 2048
H = 16
HL = 4
DK = 128
NCH = D // 128
SBK = 512
NSB = S // SBK
QT = 2048
NBLK = 4
NKT = S // 128
CKT = 16
CEXP = 16.0
EPS = 1e-5
SCL = 1.0 / np.sqrt(DK)
LORA_SCALE = 2.0
RSQRT_MAGIC = 1597463007.0


class Sem:
    def __init__(self, nc, es, name):
        self.h = es.enter_context(nc.semaphore(name))
        self.n = 0

    def reserve(self, k=1):
        self.n += k
        return self.n


class Emit:
    ENGS = ("sync", "gpsimd", "tensor", "scalar", "vector")

    def __init__(self):
        self.q = {e: [] for e in self.ENGS}

    def add(self, eng, fn):
        self.q[eng].append(fn)

    def flush(self, block):
        for e in self.ENGS:
            fns = self.q[e]
            self.q[e] = []
            if fns:
                def body(hdl, fns=fns):
                    for f in fns:
                        f(hdl)
                getattr(block, e)(body)


def build_program():
    nc = bass.Bass("TRN2", target_bir_lowering=False, debug=False)

    x_in = nc.dram_tensor("x", [S, D], F32, kind="ExternalInput")
    xq_in = nc.dram_tensor("xq", [QT, D], F32, kind="ExternalInput")
    wqT = nc.dram_tensor("wqT", [D, HL * DK], F32, kind="ExternalInput")
    wkT = nc.dram_tensor("wkT", [D, HL * DK], F32, kind="ExternalInput")
    wvT = nc.dram_tensor("wvT", [D, HL * DK], F32, kind="ExternalInput")
    woT = nc.dram_tensor("woT", [D, D], F32, kind="ExternalInput")
    wbT = nc.dram_tensor("wbT", [D, D], F32, kind="ExternalInput")
    awrT = nc.dram_tensor("awrT", [D, 72], F32, kind="ExternalInput")
    bflT = nc.dram_tensor("bflT", [64, D], F32, kind="ExternalInput")
    bbase = nc.dram_tensor("bbase", [128, 16], F32, kind="ExternalInput")
    ident_in = nc.dram_tensor("ident", [128, 128], BF16, kind="ExternalInput")
    identf_in = nc.dram_tensor("identf", [128, 128], F32, kind="ExternalInput")
    pswap_in = nc.dram_tensor("pswap", [128, 128], BF16, kind="ExternalInput")
    e8_in = nc.dram_tensor("e8", [8, 64], BF16, kind="ExternalInput")
    cc_tab = nc.dram_tensor("cctab", [128, S], BF16, kind="ExternalInput")
    ss2_tab = nc.dram_tensor("ss2tab", [128, S], BF16, kind="ExternalInput")
    masks_in = nc.dram_tensor("masks", [4 * 128, 512], BF16, kind="ExternalInput")

    out_q = nc.dram_tensor("out_q", [QT, D], F32, kind="ExternalOutput")

    qsp = [nc.dram_tensor(f"qsp{h}", [128, NSB, SBK], BF16) for h in range(HL)]
    ksp = [nc.dram_tensor(f"ksp{h}", [128, NKT, 128], BF16) for h in range(HL)]
    vsp = [nc.dram_tensor(f"vsp{h}", [128, NKT, 128], BF16) for h in range(HL)]
    ccg_in = [nc.dram_tensor(f"ccin{k}", [HL * DK, QT], BF16) for k in range(4)]
    ccg_out = [nc.dram_tensor(f"ccout{k}", [4 * HL * DK, QT], BF16) for k in range(4)]
    r1sp = nc.dram_tensor("r1sp", [QT, D], F32)

    es = contextlib.ExitStack()
    with es:
        sb = lambda name, shape, dt: es.enter_context(nc.sbuf_tensor(name, shape, dt))

        ids = sb("ids", [128, 128], BF16)
        idsf = sb("idsf", [128, 128], F32)
        psw = sb("psw", [128, 128], BF16)
        e8s = sb("e8s", [8, 64], BF16)
        onecol = sb("onecol", [128, 1], BF16)
        onecf = sb("onecf", [128, 1], F32)
        onerow = sb("onerow", [1, 128], F32)
        cexp = sb("cexp", [128, 1], F32)
        bnst = sb("bnst", [128, 24], F32)
        mv = sb("mv", [128, 2], F32)
        rstd = sb("rstd", [128, 1], F32)
        rs_t1 = sb("rs_t1", [128, 1], F32)
        rs_t2 = sb("rs_t2", [128, 1], F32)

        s_v = Sem(nc, es, "s_v")
        s_cst = Sem(nc, es, "s_cst")
        s_ldx = Sem(nc, es, "s_ldx")
        s_wld = Sem(nc, es, "s_wld")
        s_x = [Sem(nc, es, f"s_x{i}") for i in range(2)]
        s_spq = [Sem(nc, es, f"s_spq{i}") for i in range(2)]
        s_spv = [Sem(nc, es, f"s_spv{i}") for i in range(2)]
        s_q = [Sem(nc, es, f"s_q{i}") for i in range(2)]
        s_kv = [Sem(nc, es, f"s_kv{i}") for i in range(2)]
        s_sto = [Sem(nc, es, f"s_sto{i}") for i in range(2)]
        s_xq = [Sem(nc, es, f"s_xq{i}") for i in range(2)]
        s_sr = [Sem(nc, es, f"s_sr{i}") for i in range(2)]
        s_ra = [Sem(nc, es, f"s_ra{i}") for i in range(2)]
        s_rc = [Sem(nc, es, f"s_rc{i}") for i in range(2)]
        s_so = [Sem(nc, es, f"s_so{i}") for i in range(2)]
        s_ln = Sem(nc, es, "s_ln")
        s_ln2 = Sem(nc, es, "s_ln2")
        s_tp = Sem(nc, es, "s_tp")
        s_evT = Sem(nc, es, "s_evT")
        s_proj = Sem(nc, es, "s_proj")
        s_pe = Sem(nc, es, "s_pe")
        s_sw = Sem(nc, es, "s_sw")
        s_rope = Sem(nc, es, "s_rope")
        s_vev = Sem(nc, es, "s_vev")
        s_mm = Sem(nc, es, "s_mm")
        s_exp = Sem(nc, es, "s_exp")
        s_msk = Sem(nc, es, "s_msk")
        s_av = Sem(nc, es, "s_av")
        s_zi = Sem(nc, es, "s_zi")
        s_bc = Sem(nc, es, "s_bc")
        s_o = Sem(nc, es, "s_o")
        s_cc = Sem(nc, es, "s_cc")
        s_lda = Sem(nc, es, "s_lda")
        s_mm3 = Sem(nc, es, "s_mm3")
        s_yev = Sem(nc, es, "s_yev")
        s_r1 = Sem(nc, es, "s_r1")
        s_g = Sem(nc, es, "s_g")
        s_rs = Sem(nc, es, "s_rs")
        s_zd = Sem(nc, es, "s_zd")
        s_zc = Sem(nc, es, "s_zc")

        def queue_rsqrt(E):
            r = [s_v.reserve() for _ in range(9)]
            dn = s_rs.reserve()

            def body(v, r=r, dn=dn):
                v.tensor_scalar(out=rs_t1[:, :], in0=mv[:, 1:2], scalar1=EPS,
                                scalar2=None, op0=ALU.add).then_inc(s_v.h, 1)
                v.wait_ge(s_v.h, r[0])
                v.tensor_scalar(out=rs_t2[:, :], in0=rs_t1[:, :], scalar1=-0.5,
                                scalar2=None, op0=ALU.mult).then_inc(s_v.h, 1)
                v.wait_ge(s_v.h, r[1])
                v.tensor_scalar(out=rstd[:, :].bitcast(I32),
                                in0=rs_t1[:, :].bitcast(I32), scalar1=1,
                                scalar2=None,
                                op0=ALU.logical_shift_right).then_inc(s_v.h, 1)
                v.wait_ge(s_v.h, r[2])
                v.tensor_scalar(out=rstd[:, :].bitcast(I32),
                                in0=rstd[:, :].bitcast(I32), scalar1=-1.0,
                                scalar2=RSQRT_MAGIC, op0=ALU.mult,
                                op1=ALU.add).then_inc(s_v.h, 1)
                v.wait_ge(s_v.h, r[3])
                k = 4
                for it in range(2):
                    v.tensor_tensor(rs_t1[:, :], rstd[:, :], rstd[:, :],
                                    op=ALU.mult).then_inc(s_v.h, 1)
                    v.wait_ge(s_v.h, r[k]); k += 1
                    v.tensor_scalar(out=rs_t1[:, :], in0=rs_t1[:, :],
                                    scalar1=rs_t2[:, :], scalar2=1.5,
                                    op0=ALU.mult, op1=ALU.add).then_inc(s_v.h, 1)
                    v.wait_ge(s_v.h, r[k]); k += 1
                    inst = v.tensor_tensor(rstd[:, :], rstd[:, :], rs_t1[:, :],
                                           op=ALU.mult)
                    if it == 1:
                        inst.then_inc(s_rs.h, 1)
                    else:
                        inst.then_inc(s_v.h, 1)
                        v.wait_ge(s_v.h, r[k]); k += 1
            E.add("vector", body)
            return dn

        def queue_ln(E, src_fn, out_fn):
            t_stats = s_v.reserve()
            t_aggr = s_ln.reserve()

            def v_stats(v, src_fn=src_fn, t_stats=t_stats, t_aggr=t_aggr):
                src = src_fn()
                for i in range(3):
                    v.bn_stats(bnst[:, i * 6:(i + 1) * 6], src[:, i * 512:(i + 1) * 512])
                v.bn_stats(bnst[:, 18:24], src[:, 1536:2048]).then_inc(s_v.h, 1)
                v.wait_ge(s_v.h, t_stats)
                v.bn_aggr(mv[:, :], bnst[:, :]).then_inc(s_ln.h, 1)
                v.wait_ge(s_ln.h, t_aggr)
            E.add("vector", v_stats)
            dn = queue_rsqrt(E)
            t_norm = s_ln2.reserve()

            def v_norm(v, src_fn=src_fn, out_fn=out_fn, dn=dn):
                v.wait_ge(s_rs.h, dn)
                v.tensor_scalar(out=out_fn(), in0=src_fn(),
                                scalar1=mv[:, 0:1], scalar2=rstd[:, :],
                                op0=ALU.subtract, op1=ALU.mult).then_inc(s_ln2.h, 1)
            E.add("vector", v_norm)
            return t_norm

        with contextlib.ExitStack() as es1:
            sb1 = lambda n_, sh, dt: es1.enter_context(nc.sbuf_tensor(n_, sh, dt))
            ps1 = lambda n_, sh, dt: es1.enter_context(nc.psum_tensor(n_, sh, dt))
            xs = sb1("xs", [128, 2, D], F32)
            lnb = sb1("lnb", [128, D], BF16)
            lnT = sb1("lnT", [128, NCH, SBK], BF16)
            wq_s = sb1("wq_s", [128, NCH, HL * DK], BF16)
            wk_s = sb1("wk_s", [128, NCH, HL * DK], BF16)
            wv_s = sb1("wv_s", [128, NCH, HL * DK], BF16)
            ccs = sb1("ccs", [128, S], BF16)
            ss2s = sb1("ss2s", [128, S], BF16)
            pre = sb1("pre", [128, SBK], BF16)
            swp = sb1("swp", [128, SBK], BF16)
            tmpA = sb1("tmpA", [128, SBK], BF16)
            ropo = sb1("ropo", [128, 2, SBK], BF16)
            vstg = sb1("vstg", [128, 2, HL * DK], BF16)
            pT = ps1("pT", [128, 1024], BF16)
            pPJ0 = ps1("pPJ0", [128, 512], F32)
            pPJ1 = ps1("pPJ1", [128, 512], F32)
            pSw = ps1("pSw", [128, 512], F32)
            pPJ = [pPJ0, pPJ1]

            blk1 = es1.enter_context(nc.Block("p1"))
            E = Emit()

            n_ldi1 = s_ldx.reserve(16 * 6)

            def ld_init(sync):
                sync.dma_start(out=ids[:, :], in_=ident_in[:, :]).then_inc(s_ldx.h, 16)
                sync.dma_start(out=idsf[:, :], in_=identf_in[:, :]).then_inc(s_ldx.h, 16)
                sync.dma_start(out=psw[:, :], in_=pswap_in[:, :]).then_inc(s_ldx.h, 16)
                sync.dma_start(out=e8s[:, :], in_=e8_in[:, :]).then_inc(s_ldx.h, 16)
                sync.dma_start(out=ccs[:, :], in_=cc_tab[:, :]).then_inc(s_ldx.h, 16)
                sync.dma_start(out=ss2s[:, :], in_=ss2_tab[:, :]).then_inc(s_ldx.h, 16)
            E.add("sync", ld_init)

            n_wld1 = s_wld.reserve(16 * 3 * NCH)

            def ld_w(g):
                for w_d, w_s in ((wqT, wq_s), (wkT, wk_s), (wvT, wv_s)):
                    w_r = w_d.rearrange("(c p) o -> c p o", p=128)
                    for c in range(NCH):
                        g.dma_start(out=w_s[:, c, :], in_=w_r[c]).then_inc(s_wld.h, 16)
            E.add("gpsimd", ld_w)

            n_cst = s_cst.reserve()

            def consts(v):
                v.memset(onecol[:, :], 1.0)
                v.memset(onecf[:, :], 1.0)
                v.memset(onerow[:, :], 1.0)
                v.memset(cexp[:, :], -CEXP).then_inc(s_cst.h, 1)
                v.wait_ge(s_ldx.h, n_ldi1)
                v.wait_ge(s_wld.h, n_wld1)
            E.add("vector", consts)

            def eng_gate1(e, a=n_cst, b_=n_ldi1, c_=n_wld1):
                e.wait_ge(s_cst.h, a)
                e.wait_ge(s_ldx.h, b_)
                e.wait_ge(s_wld.h, c_)
            E.add("scalar", eng_gate1)
            E.add("tensor", eng_gate1)

            x_r = x_in.rearrange("(g p) d -> g p d", p=128)
            ln2_mark = []
            tp_mark = []
            for s_i in range(NSB):
                for tt in range(4):
                    g = 4 * s_i + tt
                    t_x = s_x[g % 2].reserve(16)
                    need_gx = ln2_mark[g - 2] if g >= 2 else 0

                    def ld_x(sync, g=g, need=need_gx):
                        if need > 0:
                            sync.wait_ge(s_ln2.h, need)
                        sync.dma_start(out=xs[:, g % 2, :], in_=x_r[g]).then_inc(
                            s_x[g % 2].h, 16)
                    E.add("sync", ld_x)

                    need_tp = tp_mark[g - 1] if g >= 1 else 0

                    def v_pre(v, g=g, t_x=t_x, need_tp=need_tp):
                        v.wait_ge(s_x[g % 2].h, t_x)
                        if need_tp > 0:
                            v.wait_ge(s_tp.h, need_tp)
                    E.add("vector", v_pre)
                    queue_ln(E, (lambda g=g: xs[:, g % 2, :]), (lambda: lnb[:, :]))
                    ln2_mark.append(s_ln2.n)

                    for rnd in range(2):
                        need_ln = s_ln2.n
                        need_ev = s_evT.n
                        t_tpv = s_tp.reserve()

                        def t_tp(t, rnd=rnd, need_ln=need_ln, need_ev=need_ev):
                            if rnd == 0:
                                t.wait_ge(s_ln2.h, need_ln)
                            if need_ev > 0:
                                t.wait_ge(s_evT.h, need_ev)
                            for i in range(8):
                                c = rnd * 8 + i
                                mmi = t.transpose(pT[:, i * 128:(i + 1) * 128],
                                                  lnb[:, c * 128:(c + 1) * 128], ids[:, :])
                            mmi.then_inc(s_tp.h, 1)
                        E.add("tensor", t_tp)

                        s_evT.reserve()

                        def v_ev(v, tt=tt, rnd=rnd, need=t_tpv):
                            v.wait_ge(s_tp.h, need)
                            v.tensor_copy(
                                lnT[:, rnd * 8:(rnd + 1) * 8, tt * 128:(tt + 1) * 128],
                                pT[:, :].rearrange("p (c f) -> p c f", f=128)).then_inc(
                                s_evT.h, 1)
                        E.add("vector", v_ev)
                    tp_mark.append(s_tp.n)

                ev_all = s_evT.n

                def t_sb_gate(t, need=ev_all):
                    t.wait_ge(s_evT.h, need)
                E.add("tensor", t_sb_gate)

                for h in range(HL):
                    for which, w_s in (("q", wq_s), ("k", wk_s)):
                        pj = s_proj.n % 2
                        need_pe = s_pe.n - 1
                        t_prj = s_proj.reserve()

                        def t_proj(t, h=h, w_s=w_s, pj=pj, need_pe=need_pe):
                            if need_pe > 0:
                                t.wait_ge(s_pe.h, need_pe)
                            for c in range(NCH):
                                mmi = t.matmul(pPJ[pj][:, :], w_s[:, c, h * 128:(h + 1) * 128],
                                               lnT[:, c, :], start=(c == 0),
                                               stop=(c == NCH - 1))
                            mmi.then_inc(s_proj.h, 1)
                        E.add("tensor", t_proj)

                        need_rope = s_rope.n
                        t_pev = s_pe.reserve()

                        def v_pe(v, need=t_prj, pj=pj, need_rope=need_rope):
                            v.wait_ge(s_proj.h, need)
                            if need_rope > 0:
                                v.wait_ge(s_rope.h, need_rope)
                            v.tensor_copy(pre[:, :], pPJ[pj][:, :]).then_inc(s_pe.h, 1)
                        E.add("vector", v_pe)

                        need_rp = s_rope.n
                        t_swv = s_sw.reserve()

                        def t_sw(t, need=t_pev, need_rp=need_rp):
                            t.wait_ge(s_pe.h, need)
                            if need_rp > 0:
                                t.wait_ge(s_rope.h, need_rp)
                            t.matmul(pSw[:, :], psw[:, :], pre[:, :],
                                     start=True, stop=True).then_inc(s_sw.h, 1)
                        E.add("tensor", t_sw)

                        ridx = s_rope.n
                        rslot = ridx % 2
                        va = s_v.reserve(2)
                        need_wr = s_spq[rslot].n if ridx >= 2 else 0
                        t_rp = s_rope.reserve()

                        def v_rope(v, s_i=s_i, rslot=rslot, need=t_swv, va=va,
                                   need_wr=need_wr):
                            v.wait_ge(s_sw.h, need)
                            v.tensor_copy(swp[:, :], pSw[:, :])
                            v.tensor_tensor(tmpA[:, :], pre[:, :],
                                            ccs[:, s_i * SBK:(s_i + 1) * SBK],
                                            op=ALU.mult).then_inc(s_v.h, 1)
                            v.wait_ge(s_v.h, va - 1)
                            v.tensor_tensor(swp[:, :], swp[:, :],
                                            ss2s[:, s_i * SBK:(s_i + 1) * SBK],
                                            op=ALU.mult).then_inc(s_v.h, 1)
                            v.wait_ge(s_v.h, va)
                            if need_wr > 0:
                                v.wait_ge(s_spq[rslot].h, need_wr)
                            v.tensor_tensor(ropo[:, rslot, :], tmpA[:, :],
                                            swp[:, :], op=ALU.add).then_inc(s_rope.h, 1)
                        E.add("vector", v_rope)

                        s_spq[rslot].reserve(16)

                        def sc_spill(scl, s_i=s_i, h=h, which=which, rslot=rslot,
                                     need=t_rp):
                            scl.wait_ge(s_rope.h, need)
                            if which == "q":
                                dst = qsp[h][:, s_i, :]
                            else:
                                dst = ksp[h][:, 4 * s_i:4 * s_i + 4, :].rearrange(
                                    "p a b -> p (a b)")
                            scl.dma_start(out=dst, in_=ropo[:, rslot, :]).then_inc(
                                s_spq[rslot].h, 16)
                        E.add("scalar", sc_spill)

                for tt in range(4):
                    g = 4 * s_i + tt
                    pj = s_proj.n % 2
                    need_pe = s_pe.n - 1
                    t_vp = s_proj.reserve()

                    def t_v(t, tt=tt, pj=pj, need_pe=need_pe):
                        if need_pe > 0:
                            t.wait_ge(s_pe.h, need_pe)
                        for c in range(NCH):
                            mmi = t.matmul(pPJ[pj][:, :], lnT[:, c, tt * 128:(tt + 1) * 128],
                                           wv_s[:, c, :], start=(c == 0), stop=(c == NCH - 1))
                        mmi.then_inc(s_proj.h, 1)
                    E.add("tensor", t_v)

                    vslot = g % 2
                    need_wr = s_spv[vslot].n if g >= 2 else 0
                    t_vev = s_pe.reserve()

                    def v_vev(v, vslot=vslot, pj=pj, need=t_vp, need_wr=need_wr):
                        v.wait_ge(s_proj.h, need)
                        if need_wr > 0:
                            v.wait_ge(s_spv[vslot].h, need_wr)
                        v.tensor_copy(vstg[:, vslot, :], pPJ[pj][:, :]).then_inc(s_pe.h, 1)
                    E.add("vector", v_vev)

                    s_spv[vslot].reserve(16 * HL)

                    def sc_vsp(scl, g=g, vslot=vslot, need=t_vev):
                        scl.wait_ge(s_pe.h, need)
                        for h in range(HL):
                            scl.dma_start(
                                out=vsp[h][:, g, :],
                                in_=vstg[:, vslot, h * 128:(h + 1) * 128]).then_inc(
                                s_spv[vslot].h, 16)
                    E.add("scalar", sc_vsp)

            def p1_fin(scl, a0=s_spq[0].n, a1=s_spq[1].n, b0=s_spv[0].n, b1=s_spv[1].n):
                scl.wait_ge(s_spq[0].h, a0)
                scl.wait_ge(s_spq[1].h, a1)
                scl.wait_ge(s_spv[0].h, b0)
                scl.wait_ge(s_spv[1].h, b1)
            E.add("scalar", p1_fin)
            E.flush(blk1)

        with contextlib.ExitStack() as es2:
            sb2 = lambda n_, sh, dt: es2.enter_context(nc.sbuf_tensor(n_, sh, dt))
            ps2 = lambda n_, sh, dt: es2.enter_context(nc.psum_tensor(n_, sh, dt))
            msk = sb2("msk", [128, 4, 512], BF16)
            qbuf = sb2("qbuf", [128, 2, 512], BF16)
            kbuf = sb2("kbuf", [128, 2, CKT * 128], BF16)
            vbuf = sb2("vbuf", [128, 2, CKT * 128], BF16)
            pb = sb2("pb", [128, 4, 512], BF16)
            zp = sb2("zp", [128, 4, 512], F32)
            zinv = sb2("zinv", [1, 512], F32)
            binv = sb2("binv", [128, 512], F32)
            oout = sb2("oout", [128, 2, 512], BF16)
            pS0 = ps2("pS0", [128, 512], F32)
            pS1 = ps2("pS1", [128, 512], F32)
            pO = ps2("pO", [128, 512], F32)
            pZ = ps2("pZ", [128, 512], F32)
            pB = ps2("pB", [128, 512], F32)
            pSb = [pS0, pS1]

            blk2 = es2.enter_context(nc.Block("p2"))
            E = Emit()

            n_mskld = s_ldx.reserve(16 * 4)

            def ld_msk(sync):
                m_r = masks_in.rearrange("(t p) q -> t p q", p=128)
                for i in range(4):
                    sync.dma_start(out=msk[:, i, :], in_=m_r[i]).then_inc(s_ldx.h, 16)
            E.add("sync", ld_msk)

            def eng_gate2(e, need=n_mskld):
                e.wait_ge(s_ldx.h, need)
            E.add("vector", eng_gate2)
            E.add("tensor", eng_gate2)
            E.add("scalar", eng_gate2)

            mm_mark_iter = []
            kcons_mark = []
            gchunk = 0
            for qb in range(NSB):
                nkt = 4 * (qb + 1)
                nchunks = (nkt + CKT - 1) // CKT
                for h in range(HL):
                    it = qb * HL + h
                    qslot = it % 2
                    need_qg = mm_mark_iter[it - 2] if it >= 2 else 0
                    q_mark = s_q[qslot].reserve(16)

                    def ld_q(sync, qb=qb, h=h, qslot=qslot, need=need_qg):
                        if need > 0:
                            sync.wait_ge(s_mm.h, need)
                        sync.dma_start(out=qbuf[:, qslot, :],
                                       in_=qsp[h][:, qb, :]).then_inc(s_q[qslot].h, 16)
                    E.add("sync", ld_q)

                    ch_base = gchunk
                    ld_marks = {}

                    def emit_ld_kv(ch, qb=qb, h=h, nkt=nkt, ch_base=ch_base,
                                   ld_marks=ld_marks):
                        nonlocal gchunk
                        k0 = ch * CKT
                        nk = min(CKT, nkt - k0)
                        gc = ch_base + ch
                        cs = gc % 2
                        need = kcons_mark[gc - 2] if gc >= 2 else 0
                        ld_marks[ch] = s_kv[cs].reserve(32)

                        def ld_kv(sync, h=h, k0=k0, nk=nk, cs=cs, need=need):
                            if need > 0:
                                sync.wait_ge(s_av.h, need)
                            sync.dma_start(
                                out=kbuf[:, cs, 0:nk * 128],
                                in_=ksp[h][:, k0:k0 + nk, :].rearrange(
                                    "p a b -> p (a b)")).then_inc(s_kv[cs].h, 16)
                            sync.dma_start(
                                out=vbuf[:, cs, 0:nk * 128],
                                in_=vsp[h][:, k0:k0 + nk, :].rearrange(
                                    "p a b -> p (a b)")).then_inc(s_kv[cs].h, 16)
                        E.add("sync", ld_kv)
                        gchunk += 1

                    emit_ld_kv(0)
                    if nchunks > 1:
                        emit_ld_kv(1)

                    o_guard = s_o.n
                    zd_base = s_zd.n
                    exp_base = s_exp.n

                    def emit_t_s(kt):
                        ch = kt // CKT
                        cs = (ch_base + ch) % 2
                        ko = kt - ch * CKT
                        sslot = kt % 2
                        need_exp = s_exp.n - 1
                        t_sv = s_mm.reserve()

                        def t_s(t, kt=kt, cs=cs, ko=ko, sslot=sslot, qslot=qslot,
                                need_ld=ld_marks[ch], need_exp=need_exp,
                                o_guard=o_guard, q_mark=q_mark):
                            if kt == 0:
                                if o_guard > 0:
                                    t.wait_ge(s_o.h, o_guard)
                                t.wait_ge(s_q[qslot].h, q_mark)
                                t.wait_ge(s_kv[cs].h, need_ld)
                            elif ko == 0:
                                t.wait_ge(s_kv[cs].h, need_ld)
                            if need_exp > 0:
                                t.wait_ge(s_exp.h, need_exp)
                            t.matmul(pSb[sslot][:, :],
                                     kbuf[:, cs, ko * 128:(ko + 1) * 128],
                                     qbuf[:, qslot, :], start=True,
                                     stop=True).then_inc(s_mm.h, 1)
                        E.add("tensor", t_s)
                        return t_sv

                    ts_marks = {0: emit_t_s(0)}

                    for kt in range(nkt):
                        ch = kt // CKT
                        cs = (ch_base + ch) % 2
                        ko = kt - ch * CKT
                        sslot = kt % 2
                        pslot = kt % 4
                        diag = kt >= 4 * qb
                        lastk = (ko == CKT - 1 or kt == nkt - 1)

                        need_av = s_av.n - 3
                        need_zd = s_zd.n - 3
                        t_ev = s_exp.reserve()

                        def s_e(scl, pslot=pslot, sslot=sslot, need_mm=ts_marks[kt],
                                need_av=need_av, need_zd=need_zd):
                            scl.wait_ge(s_mm.h, need_mm)
                            if need_av > 0:
                                scl.wait_ge(s_av.h, need_av)
                            if need_zd > 0:
                                scl.wait_ge(s_zd.h, need_zd)
                            scl.activation(out=pb[:, pslot, :], in_=pSb[sslot][:, :],
                                           func=AF.Exp, bias=cexp[:, :],
                                           scale=SCL).then_inc(s_exp.h, 1)
                        E.add("scalar", s_e)

                        if diag:
                            t_mk = s_msk.reserve()

                            def v_m(v, qb=qb, kt=kt, pslot=pslot, need=t_ev):
                                v.wait_ge(s_exp.h, need)
                                v.tensor_tensor(pb[:, pslot, :], pb[:, pslot, :],
                                                msk[:, kt - 4 * qb, :],
                                                op=ALU.mult).then_inc(s_msk.h, 1)
                            E.add("vector", v_m)
                        need_m = s_msk.n

                        need_zslot = s_zd.n - 3
                        t_zd = s_zd.reserve()

                        def v_z(v, kt=kt, pslot=pslot, diag=diag, need_e=t_ev,
                                need_m=need_m, need_zslot=need_zslot):
                            if diag:
                                v.wait_ge(s_msk.h, need_m)
                            else:
                                v.wait_ge(s_exp.h, need_e)
                            if kt >= 4:
                                v.wait_ge(s_zd.h, need_zslot)
                                v.tensor_tensor(zp[:, kt % 4, :], zp[:, kt % 4, :],
                                                pb[:, pslot, :],
                                                op=ALU.add).then_inc(s_zd.h, 1)
                            else:
                                v.tensor_copy(zp[:, kt % 4, :],
                                              pb[:, pslot, :]).then_inc(s_zd.h, 1)
                        E.add("vector", v_z)

                        if kt + 1 < nkt:
                            ts_marks[kt + 1] = emit_t_s(kt + 1)

                        t_avv = s_av.reserve()

                        def t_av(t, kt=kt, cs=cs, ko=ko, pslot=pslot, diag=diag,
                                 need_e=t_ev, need_m=need_m, nkt=nkt):
                            if diag:
                                t.wait_ge(s_msk.h, need_m)
                            else:
                                t.wait_ge(s_exp.h, need_e)
                            t.matmul(pO[:, :], vbuf[:, cs, ko * 128:(ko + 1) * 128],
                                     pb[:, pslot, :], start=(kt == 0),
                                     stop=(kt == nkt - 1)).then_inc(s_av.h, 1)
                        E.add("tensor", t_av)
                        if lastk:
                            kcons_mark.append(s_av.n)
                            if ch + 2 < nchunks:
                                emit_ld_kv(ch + 2)

                    mm_mark_iter.append(s_mm.n)

                    zd_all = s_zd.n
                    va = s_v.reserve(2)
                    t_zcmb = s_zd.reserve()

                    def v_zcmb(v, zd_all=zd_all, va=va):
                        v.wait_ge(s_zd.h, zd_all)
                        v.tensor_tensor(zp[:, 0, :], zp[:, 0, :], zp[:, 1, :],
                                        op=ALU.add).then_inc(s_v.h, 1)
                        v.tensor_tensor(zp[:, 2, :], zp[:, 2, :], zp[:, 3, :],
                                        op=ALU.add).then_inc(s_v.h, 1)
                        v.wait_ge(s_v.h, va)
                        v.tensor_tensor(zp[:, 0, :], zp[:, 0, :], zp[:, 2, :],
                                        op=ALU.add).then_inc(s_zd.h, 1)
                    E.add("vector", v_zcmb)

                    t_zcv = s_zc.reserve()

                    def t_zc(t, need=t_zcmb):
                        t.wait_ge(s_zd.h, need)
                        t.matmul(pZ[0:1, :], onecf[:, :], zp[:, 0, :],
                                 start=True, stop=True).then_inc(s_zc.h, 1)
                    E.add("tensor", t_zc)

                    t_ziv = s_zi.reserve()

                    def v_zi(v, need_zc=t_zcv, need_av=s_av.n):
                        v.wait_ge(s_zc.h, need_zc)
                        v.wait_ge(s_av.h, need_av)
                        v.reciprocal(zinv[:, :], pZ[0:1, :]).then_inc(s_zi.h, 1)
                    E.add("vector", v_zi)

                    t_bcv = s_bc.reserve()

                    def t_bc(t, need=t_ziv):
                        t.wait_ge(s_zi.h, need)
                        t.matmul(pB[:, :], onerow[:, :], zinv[:, :],
                                 start=True, stop=True).then_inc(s_bc.h, 1)
                    E.add("tensor", t_bc)

                    oslot = it % 2
                    need_st = s_sto[oslot].n if it >= 2 else 0
                    va = s_v.reserve()
                    t_ov = s_o.reserve()

                    def v_o(v, oslot=oslot, need_bc=t_bcv, need_st=need_st, va=va):
                        v.wait_ge(s_bc.h, need_bc)
                        v.tensor_copy(binv[:, :], pB[:, :]).then_inc(s_v.h, 1)
                        if need_st > 0:
                            v.wait_ge(s_sto[oslot].h, need_st)
                        v.wait_ge(s_v.h, va)
                        v.tensor_tensor(oout[:, oslot, :], pO[:, :], binv[:, :],
                                        op=ALU.mult).then_inc(s_o.h, 1)
                    E.add("vector", v_o)

                    s_sto[oslot].reserve(16)

                    def sc_o(scl, qb=qb, h=h, oslot=oslot, need=t_ov):
                        scl.wait_ge(s_o.h, need)
                        k = qb // 4
                        scl.dma_start(
                            out=ccg_in[k][h * 128:(h + 1) * 128,
                                          (qb % 4) * 512:(qb % 4 + 1) * 512],
                            in_=oout[:, oslot, :]).then_inc(s_sto[oslot].h, 16)
                    E.add("scalar", sc_o)

                if qb % 4 == 3:
                    s_cc.reserve()

                    def g_cc(g, k=qb // 4, need0=s_sto[0].n, need1=s_sto[1].n):
                        g.wait_ge(s_sto[0].h, need0)
                        g.wait_ge(s_sto[1].h, need1)
                        g.collective_compute(
                            "AllGather", ALU.bypass,
                            replica_groups=[[0, 1, 2, 3], [4, 5, 6, 7]],
                            ins=[ccg_in[k].ap().opt()],
                            outs=[ccg_out[k].ap().opt()]).then_inc(s_cc.h, 1)
                    E.add("gpsimd", g_cc)
            E.flush(blk2)

        with contextlib.ExitStack() as es3:
            sb3 = lambda n_, sh, dt: es3.enter_context(nc.sbuf_tensor(n_, sh, dt))
            ps3 = lambda n_, sh, dt: es3.enter_context(nc.psum_tensor(n_, sh, dt))
            wo_s = sb3("wo_s", [128, NCH, D], BF16)
            aT = sb3("aT", [128, NCH, 512], BF16)
            y1T = sb3("y1T", [128, NCH, 512], BF16)
            y1 = sb3("y1", [128, 4, D], BF16)
            xqb = sb3("xqb", [128, 2, D], F32)
            t3 = sb3("t3", [128, D], F32)
            r1b = sb3("r1b", [128, 2, D], F32)
            pY = ps3("pY", [128, 512], F32)
            pT3 = ps3("pT3", [128, 1024], BF16)

            blk3 = es3.enter_context(nc.Block("p3"))
            E = Emit()

            n_wld3 = s_wld.reserve(16 * NCH)

            def ld_wo(g):
                w_r = woT.rearrange("(c p) o -> c p o", p=128)
                for c in range(NCH):
                    g.dma_start(out=wo_s[:, c, :], in_=w_r[c]).then_inc(s_wld.h, 16)
            E.add("gpsimd", ld_wo)

            def t_gate3(t, need=n_wld3):
                t.wait_ge(s_wld.h, need)
            E.add("tensor", t_gate3)

            xq_r = xq_in.rearrange("(g p) d -> g p d", p=128)
            r1_r = r1sp.rearrange("(g p) d -> g p d", p=128)

            r1_mark = []
            mm3_mark = []
            for tb in range(NBLK):
                E.flush(blk3)
                need_mm = mm3_mark[tb - 1] if tb >= 1 else 0
                lda_tb = s_lda.reserve(16 * NCH)

                def sync_aT(sync, tb=tb, need_mm=need_mm):
                    if need_mm > 0:
                        sync.wait_ge(s_mm3.h, need_mm)
                    pid = sync.partition_id()
                    for core in nc.Switch(engines=[sync], index=[pid], n=8):
                        tq = core % 4
                        sync.wait_ge(s_cc.h, tq + 1)
                        co = ccg_out[tq]
                        for cb in range(NCH):
                            sync.dma_start(
                                out=aT[:, cb, :],
                                in_=co[cb * 128:(cb + 1) * 128,
                                       tb * 512:(tb + 1) * 512]).then_inc(s_lda.h, 16)
                blk3.sync(sync_aT)

                for oc in range(NCH):
                    need_ev = s_yev.n
                    t_mmv = s_mm3.reserve()

                    def t_op(t, oc=oc, need_ld=lda_tb, need_ev=need_ev):
                        if oc == 0:
                            t.wait_ge(s_lda.h, need_ld)
                        if need_ev > 0:
                            t.wait_ge(s_yev.h, need_ev)
                        for c in range(NCH):
                            mmi = t.matmul(pY[:, :], wo_s[:, c, oc * 128:(oc + 1) * 128],
                                           aT[:, c, :], start=(c == 0), stop=(c == NCH - 1))
                        mmi.then_inc(s_mm3.h, 1)
                    E.add("tensor", t_op)

                    s_yev.reserve()

                    def v_ye(v, oc=oc, need=t_mmv):
                        v.wait_ge(s_mm3.h, need)
                        v.tensor_copy(y1T[:, oc, :], pY[:, :]).then_inc(s_yev.h, 1)
                    E.add("vector", v_ye)
                mm3_mark.append(s_mm3.n)

                for rnd in range(8):
                    need_ev = s_yev.n
                    need_pt = s_evT.n
                    t_tpv = s_tp.reserve()

                    def t_tp3(t, rnd=rnd, need_ev=need_ev, need_pt=need_pt):
                        if rnd == 0:
                            t.wait_ge(s_yev.h, need_ev)
                        if need_pt > 0:
                            t.wait_ge(s_evT.h, need_pt)
                        for i in range(8):
                            j = rnd * 8 + i
                            oc, sub = j // 4, j % 4
                            mmi = t.transpose(pT3[:, i * 128:(i + 1) * 128],
                                              y1T[:, oc, sub * 128:(sub + 1) * 128],
                                              ids[:, :])
                        mmi.then_inc(s_tp.h, 1)
                    E.add("tensor", t_tp3)

                    s_evT.reserve()

                    def v_pt3(v, rnd=rnd, need=t_tpv):
                        v.wait_ge(s_tp.h, need)
                        oc0 = rnd * 2
                        v.tensor_copy(
                            y1[:, :, oc0 * 128:(oc0 + 2) * 128].rearrange(
                                "p s (c f) -> p c s f", f=128),
                            pT3[:, :].rearrange("p (c s f) -> p c s f",
                                                c=2, s=4)).then_inc(s_evT.h, 1)
                    E.add("vector", v_pt3)

                for sub in range(4):
                    g = tb * 4 + sub
                    need_r1g = r1_mark[g - 2] if g >= 2 else 0
                    t_xqv = s_xq[g % 2].reserve(16)

                    def ld_xq(sync, g=g, need=need_r1g):
                        if need > 0:
                            sync.wait_ge(s_r1.h, need)
                        sync.dma_start(out=xqb[:, g % 2, :], in_=xq_r[g]).then_inc(
                            s_xq[g % 2].h, 16)
                    E.add("sync", ld_xq)

                    need_t = s_evT.n

                    def v_lngate(v, need_t=need_t):
                        v.wait_ge(s_evT.h, need_t)
                    E.add("vector", v_lngate)
                    tnorm = queue_ln(E, (lambda sub=sub: y1[:, sub, :]),
                                     (lambda: t3[:, :]))

                    need_wr = s_sr[g % 2].n if g >= 2 else 0
                    t_r1v = s_r1.reserve()

                    def v_r1(v, g=g, need_ld=t_xqv, need_wr=need_wr, tnorm=tnorm):
                        v.wait_ge(s_xq[g % 2].h, need_ld)
                        v.wait_ge(s_ln2.h, tnorm)
                        if need_wr > 0:
                            v.wait_ge(s_sr[g % 2].h, need_wr)
                        v.tensor_tensor(r1b[:, g % 2, :], t3[:, :],
                                        xqb[:, g % 2, :], op=ALU.add).then_inc(s_r1.h, 1)
                    E.add("vector", v_r1)
                    r1_mark.append(s_r1.n)

                    s_sr[g % 2].reserve(16)

                    def sc_r1(scl, g=g, need=t_r1v):
                        scl.wait_ge(s_r1.h, need)
                        scl.dma_start(out=r1_r[g], in_=r1b[:, g % 2, :]).then_inc(
                            s_sr[g % 2].h, 16)
                    E.add("scalar", sc_r1)

            def p3_fin(scl, a0=s_sr[0].n, a1=s_sr[1].n):
                scl.wait_ge(s_sr[0].h, a0)
                scl.wait_ge(s_sr[1].h, a1)
            E.add("scalar", p3_fin)
            E.flush(blk3)

        with contextlib.ExitStack() as es4:
            sb4 = lambda n_, sh, dt: es4.enter_context(nc.sbuf_tensor(n_, sh, dt))
            ps4 = lambda n_, sh, dt: es4.enter_context(nc.psum_tensor(n_, sh, dt))
            wb_s = sb4("wb_s", [128, NCH, D], BF16)
            awr_s = sb4("awr_s", [128, NCH, 72], BF16)
            bfl_s = sb4("bfl_s", [64, D], BF16)
            bb_s = sb4("bb_s", [128, 16], F32)
            r1a = sb4("r1a", [128, 2, D], F32)
            r1c = sb4("r1c", [128, 2, D], F32)
            ln2 = sb4("ln2", [128, D], BF16)
            ln2T = sb4("ln2T", [128, NCH, 512], BF16)
            hbuf = sb4("hbuf", [128, 512], BF16)
            hbufL = sb4("hbufL", [8, 512], F32)
            lgt = sb4("lgt", [128, 4, 8], F32)
            negm1 = sb4("negm1", [128, 1], F32)
            m2b = sb4("m2b", [128, 1], F32)
            eqb = sb4("eqb", [128, 8], F32)
            lmk = sb4("lmk", [128, 8], F32)
            pexp = sb4("pexp", [128, 8], F32)
            indb = sb4("indb", [128, 8], F32)
            wgt = sb4("wgt", [128, 8], F32)
            zg = sb4("zg", [128, 1], F32)
            zgi = sb4("zgi", [128, 1], F32)
            wn = sb4("wn", [128, 8], F32)
            wnT = sb4("wnT", [8, 128], BF16)
            wexpT = sb4("wexpT", [64, 512], BF16)
            h2 = sb4("h2", [64, 512], BF16)
            ffnT = sb4("ffnT", [128, NCH, 512], BF16)
            ffn = sb4("ffn", [128, 4, D], BF16)
            t4 = sb4("t4", [128, D], F32)
            r2b = sb4("r2b", [128, 2, D], F32)
            pY4 = ps4("pY4", [128, 512], F32)
            pT4 = ps4("pT4", [128, 1024], BF16)
            pH = ps4("pH", [128, 512], F32)
            pG = ps4("pG", [128, 512], F32)

            blk4 = es4.enter_context(nc.Block("p4"))
            E = Emit()

            n_wld4 = s_wld.reserve(16 * (2 * NCH + 1))

            def ld_w4(g):
                w_r = wbT.rearrange("(c p) o -> c p o", p=128)
                for c in range(NCH):
                    g.dma_start(out=wb_s[:, c, :], in_=w_r[c]).then_inc(s_wld.h, 16)
                a_r = awrT.rearrange("(c p) o -> c p o", p=128)
                for c in range(NCH):
                    g.dma_start(out=awr_s[:, c, :], in_=a_r[c]).then_inc(s_wld.h, 16)
                g.dma_start(out=bfl_s[:, :], in_=bflT[:, :]).then_inc(s_wld.h, 16)
            E.add("gpsimd", ld_w4)

            n_ld4 = s_ldx.reserve(16)

            def ld_bb(sync):
                sync.dma_start(out=bb_s[:, :], in_=bbase[:, :]).then_inc(s_ldx.h, 16)
            E.add("sync", ld_bb)

            def gate4(e, a=n_wld4, b_=n_ld4):
                e.wait_ge(s_wld.h, a)
                e.wait_ge(s_ldx.h, b_)
            E.add("tensor", gate4)
            E.add("vector", gate4)
            E.add("scalar", gate4)

            out_r = out_q.rearrange("(g p) d -> g p d", p=128)

            ln3_mark = []
            r2_mark = []
            for tb in range(NBLK):
                for sub in range(4):
                    g = tb * 4 + sub
                    need_lg = ln3_mark[g - 2] if g >= 2 else 0
                    t_rav = s_ra[g % 2].reserve(16)

                    def ld_r1(sync, g=g, need=need_lg):
                        if need > 0:
                            sync.wait_ge(s_ln2.h, need)
                        sync.dma_start(out=r1a[:, g % 2, :], in_=r1_r[g]).then_inc(
                            s_ra[g % 2].h, 16)
                    E.add("sync", ld_r1)

                    need_tp = s_tp.n

                    def v_lngate4(v, g=g, t_rav=t_rav, need_tp=need_tp):
                        v.wait_ge(s_ra[g % 2].h, t_rav)
                        if need_tp > 0:
                            v.wait_ge(s_tp.h, need_tp)
                    E.add("vector", v_lngate4)
                    queue_ln(E, (lambda g=g: r1a[:, g % 2, :]), (lambda: ln2[:, :]))
                    ln3_mark.append(s_ln2.n)

                    for rnd in range(2):
                        need_ln = s_ln2.n
                        need_ev = s_evT.n
                        t_tpv = s_tp.reserve()

                        def t_tp4(t, rnd=rnd, need_ln=need_ln, need_ev=need_ev):
                            if rnd == 0:
                                t.wait_ge(s_ln2.h, need_ln)
                            if need_ev > 0:
                                t.wait_ge(s_evT.h, need_ev)
                            for i in range(8):
                                c = rnd * 8 + i
                                mmi = t.transpose(pT4[:, i * 128:(i + 1) * 128],
                                                  ln2[:, c * 128:(c + 1) * 128], ids[:, :])
                            mmi.then_inc(s_tp.h, 1)
                        E.add("tensor", t_tp4)

                        s_evT.reserve()

                        def v_ev4(v, sub=sub, rnd=rnd, need=t_tpv):
                            v.wait_ge(s_tp.h, need)
                            v.tensor_copy(
                                ln2T[:, rnd * 8:(rnd + 1) * 8, sub * 128:(sub + 1) * 128],
                                pT4[:, :].rearrange("p (c f) -> p c f", f=128)).then_inc(
                                s_evT.h, 1)
                        E.add("vector", v_ev4)

                need_g = s_g.n
                t_awrv = s_mm3.reserve()

                def t_awr(t, need=s_evT.n, need_g=need_g):
                    t.wait_ge(s_evT.h, need)
                    if need_g > 0:
                        t.wait_ge(s_g.h, need_g)
                    for c in range(NCH):
                        mmi = t.matmul(pH[0:72, :], awr_s[:, c, :], ln2T[:, c, :],
                                       start=(c == 0), stop=(c == NCH - 1))
                    mmi.then_inc(s_mm3.h, 1)
                E.add("tensor", t_awr)

                va = s_v.reserve()
                t_gv = s_g.reserve()

                def v_hev(v, need=t_awrv, va=va):
                    v.wait_ge(s_mm3.h, need)
                    v.tensor_copy(hbuf[0:64, :], pH[0:64, :]).then_inc(s_v.h, 1)
                    v.wait_ge(s_v.h, va)
                    v.tensor_copy(hbufL[:, :], pH[64:72, :]).then_inc(s_g.h, 1)
                E.add("vector", v_hev)

                need_cons = s_lda.n
                for sub in range(4):
                    t_lgv = s_tp.reserve()

                    def t_lg(t, sub=sub, need=t_gv, need_cons=need_cons):
                        if sub == 0:
                            t.wait_ge(s_g.h, need)
                            if need_cons > 0:
                                t.wait_ge(s_lda.h, need_cons)
                        t.transpose(pG[:, sub * 8:(sub + 1) * 8],
                                    hbufL[:, sub * 128:(sub + 1) * 128],
                                    idsf[0:8, 0:8]).then_inc(s_tp.h, 1)
                    E.add("tensor", t_lg)

                lg_need = s_tp.n
                t_lgc = s_lda.reserve()

                def v_lg(v, need=lg_need):
                    v.wait_ge(s_tp.h, need)
                    v.tensor_copy(
                        lgt[:, :, :], pG[:, 0:32].rearrange(
                            "p (s e) -> p s e", e=8)).then_inc(s_lda.h, 1)
                E.add("vector", v_lg)

                for sub in range(4):
                    va = s_v.reserve(4)
                    t_ind = s_msk.reserve()

                    def v_g1(v, sub=sub, need=t_lgc, va=va):
                        if sub == 0:
                            v.wait_ge(s_lda.h, need)
                        l = lgt[:, sub, :]
                        v.reduce_max(negm1[:, :], l, axis=AX.X,
                                     negate=True).then_inc(s_v.h, 1)
                        v.wait_ge(s_v.h, va - 3)
                        v.tensor_scalar(out=eqb[:, :], in0=l, scalar1=negm1[:, :],
                                        scalar2=0.0, op0=ALU.add,
                                        op1=ALU.is_ge).then_inc(s_v.h, 1)
                        v.wait_ge(s_v.h, va - 2)
                        v.scalar_tensor_tensor(out=lmk[:, :], in0=eqb[:, :],
                                               scalar=-30000.0, in1=l,
                                               op0=ALU.mult, op1=ALU.add).then_inc(s_v.h, 1)
                        v.wait_ge(s_v.h, va - 1)
                        v.reduce_max(m2b[:, :], lmk[:, :], axis=AX.X).then_inc(s_v.h, 1)
                        v.wait_ge(s_v.h, va)
                        v.tensor_scalar(out=indb[:, :], in0=l, scalar1=m2b[:, :],
                                        scalar2=None, op0=ALU.is_ge).then_inc(s_msk.h, 1)
                    E.add("vector", v_g1)

                    t_px = s_rs.reserve()

                    def s_px(scl, sub=sub, need=t_ind):
                        scl.wait_ge(s_msk.h, need)
                        scl.activation(out=pexp[:, :], in_=lgt[:, sub, :],
                                       func=AF.Exp, bias=negm1[:, :],
                                       scale=1.0).then_inc(s_rs.h, 1)
                    E.add("scalar", s_px)

                    va = s_v.reserve(3)
                    t_wnv = s_ln2.reserve()

                    def v_g2(v, need=t_px, va=va):
                        v.wait_ge(s_rs.h, need)
                        v.tensor_tensor(wgt[:, :], indb[:, :], pexp[:, :],
                                        op=ALU.mult).then_inc(s_v.h, 1)
                        v.wait_ge(s_v.h, va - 2)
                        v.reduce_sum(zg[:, :], wgt[:, :], axis=AX.X).then_inc(s_v.h, 1)
                        v.wait_ge(s_v.h, va - 1)
                        v.reciprocal(zgi[:, :], zg[:, :]).then_inc(s_v.h, 1)
                        v.wait_ge(s_v.h, va)
                        v.tensor_scalar(out=wn[:, :], in0=wgt[:, :],
                                        scalar1=zgi[:, :], scalar2=LORA_SCALE,
                                        op0=ALU.mult, op1=ALU.mult).then_inc(s_ln2.h, 1)
                    E.add("vector", v_g2)

                    need_ev = s_yev.n
                    t_wtv = s_tp.reserve()

                    def t_wnT(t, need=t_wnv, need_ev=need_ev):
                        t.wait_ge(s_ln2.h, need)
                        if need_ev > 0:
                            t.wait_ge(s_yev.h, need_ev)
                        t.transpose(pG[0:8, 128:256], wn[:, :],
                                    idsf[:, :]).then_inc(s_tp.h, 1)
                    E.add("tensor", t_wnT)

                    t_wcv = s_r1.reserve()

                    def v_wnT(v, need=t_wtv):
                        v.wait_ge(s_tp.h, need)
                        v.tensor_copy(wnT[:, :], pG[0:8, 128:256]).then_inc(s_r1.h, 1)
                    E.add("vector", v_wnT)

                    t_e8v = s_bc.reserve()

                    def t_e8(t, need=t_wcv):
                        t.wait_ge(s_r1.h, need)
                        t.matmul(pG[0:64, 256:384], e8s[:, :], wnT[:, :],
                                 start=True, stop=True).then_inc(s_bc.h, 1)
                    E.add("tensor", t_e8)

                    s_yev.reserve()

                    def v_we(v, sub=sub, need=t_e8v):
                        v.wait_ge(s_bc.h, need)
                        v.tensor_copy(wexpT[:, sub * 128:(sub + 1) * 128],
                                      pG[0:64, 256:384]).then_inc(s_yev.h, 1)
                    E.add("vector", v_we)

                t_h2v = s_g.reserve()

                def v_h2(v, need=s_yev.n):
                    v.wait_ge(s_yev.h, need)
                    v.tensor_tensor(h2[:, :], hbuf[0:64, :], wexpT[:, :],
                                    op=ALU.mult).then_inc(s_g.h, 1)
                E.add("vector", v_h2)

                for oc in range(NCH):
                    need_ev = s_exp.n
                    t_fv = s_mm.reserve()

                    def t_ffn(t, oc=oc, need_h2=t_h2v, need_ev=need_ev):
                        if oc == 0:
                            t.wait_ge(s_g.h, need_h2)
                        if need_ev > 0:
                            t.wait_ge(s_exp.h, need_ev)
                        for c in range(NCH):
                            t.matmul(pY4[:, :], wb_s[:, c, oc * 128:(oc + 1) * 128],
                                     ln2T[:, c, :], start=(c == 0), stop=False)
                        t.matmul(pY4[:, :], bfl_s[:, oc * 128:(oc + 1) * 128],
                                 h2[:, :], start=False, stop=True).then_inc(s_mm.h, 1)
                    E.add("tensor", t_ffn)

                    s_exp.reserve()

                    def s_fev(scl, oc=oc, need=t_fv):
                        scl.wait_ge(s_mm.h, need)
                        scl.activation(out=ffnT[:, oc, :], in_=pY4[:, :],
                                       func=AF.Identity, bias=bb_s[:, oc:oc + 1],
                                       scale=1.0).then_inc(s_exp.h, 1)
                    E.add("scalar", s_fev)

                for rnd in range(8):
                    need_ev = s_exp.n
                    need_pt = s_evT.n
                    t_tpv = s_tp.reserve()

                    def t_tpf(t, rnd=rnd, need_ev=need_ev, need_pt=need_pt):
                        if rnd == 0:
                            t.wait_ge(s_exp.h, need_ev)
                        if need_pt > 0:
                            t.wait_ge(s_evT.h, need_pt)
                        for i in range(8):
                            j = rnd * 8 + i
                            oc, sub = j // 4, j % 4
                            mmi = t.transpose(pT4[:, i * 128:(i + 1) * 128],
                                              ffnT[:, oc, sub * 128:(sub + 1) * 128],
                                              ids[:, :])
                        mmi.then_inc(s_tp.h, 1)
                    E.add("tensor", t_tpf)

                    s_evT.reserve()

                    def v_ptf(v, rnd=rnd, need=t_tpv):
                        v.wait_ge(s_tp.h, need)
                        oc0 = rnd * 2
                        v.tensor_copy(
                            ffn[:, :, oc0 * 128:(oc0 + 2) * 128].rearrange(
                                "p s (c f) -> p c s f", f=128),
                            pT4[:, :].rearrange("p (c s f) -> p c s f",
                                                c=2, s=4)).then_inc(s_evT.h, 1)
                    E.add("vector", v_ptf)

                for sub in range(4):
                    g = tb * 4 + sub
                    need_r2g = r2_mark[g - 2] if g >= 2 else 0
                    t_rcv = s_rc[g % 2].reserve(16)

                    def ld_r1b(sync, g=g, need=need_r2g):
                        if need > 0:
                            sync.wait_ge(s_o.h, need)
                        sync.dma_start(out=r1c[:, g % 2, :], in_=r1_r[g]).then_inc(
                            s_rc[g % 2].h, 16)
                    E.add("sync", ld_r1b)

                    need_t = s_evT.n

                    def v_lngate5(v, need_t=need_t):
                        v.wait_ge(s_evT.h, need_t)
                    E.add("vector", v_lngate5)
                    tnorm = queue_ln(E, (lambda sub=sub: ffn[:, sub, :]),
                                     (lambda: t4[:, :]))

                    need_st = s_so[g % 2].n if g >= 2 else 0
                    t_r2v = s_o.reserve()

                    def v_r2(v, g=g, need_ld=t_rcv, need_st=need_st, tnorm=tnorm):
                        v.wait_ge(s_rc[g % 2].h, need_ld)
                        v.wait_ge(s_ln2.h, tnorm)
                        if need_st > 0:
                            v.wait_ge(s_so[g % 2].h, need_st)
                        v.tensor_tensor(r2b[:, g % 2, :], t4[:, :],
                                        r1c[:, g % 2, :], op=ALU.add).then_inc(s_o.h, 1)
                    E.add("vector", v_r2)
                    r2_mark.append(s_o.n)

                    s_so[g % 2].reserve(16)

                    def sc_out(scl, g=g, need=t_r2v):
                        scl.wait_ge(s_o.h, need)
                        scl.dma_start(out=out_r[g], in_=r2b[:, g % 2, :]).then_inc(
                            s_so[g % 2].h, 16)
                    E.add("scalar", sc_out)

            def p4_fin(scl, a0=s_so[0].n, a1=s_so[1].n):
                scl.wait_ge(s_so[0].h, a0)
                scl.wait_ge(s_so[1].h, a1)
            E.add("scalar", p4_fin)
            E.flush(blk4)

    return nc


def _host_tables():
    half = DK // 2
    inv_freq = 10000.0 ** (-2.0 * np.arange(half) / DK)
    ang = np.arange(S)[:, None] * inv_freq[None, :]
    cosT = np.cos(ang).T.astype(np.float32)
    sinT = np.sin(ang).T.astype(np.float32)
    cc = np.concatenate([cosT, cosT], axis=0).astype(ml_dtypes.bfloat16)
    ss2 = np.concatenate([-sinT, sinT], axis=0).astype(ml_dtypes.bfloat16)
    masks = np.zeros((4 * 128, 512), dtype=ml_dtypes.bfloat16)
    for i in range(4):
        for kk in range(128):
            masks[i * 128 + kk, i * 128 + kk:] = 1.0
    ident = np.eye(128, dtype=ml_dtypes.bfloat16)
    pswap = np.zeros((128, 128), dtype=ml_dtypes.bfloat16)
    for i in range(64):
        pswap[i + 64, i] = 1.0
        pswap[i, i + 64] = 1.0
    e8 = np.zeros((8, 64), dtype=ml_dtypes.bfloat16)
    for e in range(8):
        e8[e, e * 8:(e + 1) * 8] = 1.0
    return cc, ss2, masks, ident, pswap, e8


def _perm_rows(w):
    perm = []
    for h in range(H):
        base = h * DK
        perm += [base + 2 * i for i in range(DK // 2)]
        perm += [base + 2 * i + 1 for i in range(DK // 2)]
    return w[perm, :]


_NC_CACHE = {}


def kernel(L0_attn_x, L0_attn_wq, L0_attn_wk, L0_attn_wv, L0_attn_wo,
           L0_ffn_W_base, L0_ffn_b_base, L0_ffn_W_router, L0_ffn_A_w, L0_ffn_B_w):
    x = np.ascontiguousarray(np.asarray(L0_attn_x, dtype=np.float32))
    wq = np.asarray(L0_attn_wq, dtype=np.float32)
    wk = np.asarray(L0_attn_wk, dtype=np.float32)
    wv = np.asarray(L0_attn_wv, dtype=np.float32)
    wo = np.asarray(L0_attn_wo, dtype=np.float32)
    wb = np.asarray(L0_ffn_W_base, dtype=np.float32)
    bb = np.asarray(L0_ffn_b_base, dtype=np.float32)
    wr = np.asarray(L0_ffn_W_router, dtype=np.float32)
    aw = np.asarray(L0_ffn_A_w, dtype=np.float32)
    bw = np.asarray(L0_ffn_B_w, dtype=np.float32)

    cc, ss2, masks, ident, pswap, e8 = _host_tables()

    wqTf = np.ascontiguousarray(_perm_rows(wq).T)
    wkTf = np.ascontiguousarray(_perm_rows(wk).T)
    wvTf = np.ascontiguousarray(wv.T)
    woTf = np.ascontiguousarray(wo.T)
    wbTf = np.ascontiguousarray(wb.T)
    aflT = aw.reshape(64, D).T
    wrTf = wr.T
    awrTf = np.ascontiguousarray(np.concatenate([aflT, wrTf], axis=1))
    bflTf = np.ascontiguousarray(bw.transpose(0, 2, 1).reshape(64, D))
    bbasef = np.ascontiguousarray(bb.reshape(16, 128).T)

    if "nc" not in _NC_CACHE:
        _NC_CACHE["nc"] = build_program()
    nc = _NC_CACHE["nc"]

    in_maps = []
    for c in range(8):
        b = c // 4
        hg = c % 4
        tq = c % 4
        in_maps.append({
            "x": x[b],
            "xq": np.ascontiguousarray(x[b, tq * QT:(tq + 1) * QT, :]),
            "wqT": np.ascontiguousarray(wqTf[:, hg * 512:(hg + 1) * 512]),
            "wkT": np.ascontiguousarray(wkTf[:, hg * 512:(hg + 1) * 512]),
            "wvT": np.ascontiguousarray(wvTf[:, hg * 512:(hg + 1) * 512]),
            "woT": woTf, "wbT": wbTf, "awrT": awrTf, "bflT": bflTf, "bbase": bbasef,
            "ident": ident, "identf": np.eye(128, dtype=np.float32),
            "pswap": pswap, "e8": e8,
            "cctab": cc, "ss2tab": ss2, "masks": masks,
        })
    res = run_bass_kernel_spmd(nc, in_maps, core_ids=list(range(8)))
    out = np.empty((B, S, D), dtype=np.float32)
    for c in range(8):
        b = c // 4
        tq = c % 4
        out[b, tq * QT:(tq + 1) * QT, :] = res.results[c]["out_q"]
    return out
